# version 26
# baseline (speedup 1.0000x reference)
import math
import numpy as np
import ml_dtypes
import concourse.bass as bass
import concourse.mybir as mybir
from concourse.bass_utils import run_bass_kernel_spmd
import re as _re

MEASURED_COST = {
    "gpsimd|ld_dt": 617, "gpsimd|ld_gnsc": 581, "gpsimd|ld_gqE": 644, "gpsimd|ld_gqO": 635,
    "gpsimd|ld_lam": 636, "gpsimd|ld_w1": 637, "gpsimd|ld_w2": 638, "gpsimd|ld_wkT": 607,
    "gpsimd|ld_wk_bd": 649, "gpsimd|ld_wq_bd": 641, "gpsimd|ld_wrT": 644, "gpsimd|ld_wvT": 618,
    "gpsimd|ld_wv_bd": 2288, "gpsimd|sbf": 1870, "gpsimd|sdec": 1270, "scalar|at20": 777,
    "scalar|at21": 669, "scalar|gnss": 2481, "scalar|grs": 1590, "scalar|gsq0": 687,
    "scalar|gsq1": 687, "scalar|kT_es": 579, "scalar|kT_ev": 687, "scalar|kr0": 579,
    "scalar|kr1": 579, "scalar|ln1_rstd": 295, "scalar|ln1n": 1240, "scalar|ln2_rstd": 294,
    "scalar|ln2_sqA": 436, "scalar|ln2_sqB": 380, "scalar|ln3_rstd": 295, "scalar|ln3n": 1239,
    "scalar|ln4_rstd": 1578, "scalar|ln4n": 1239, "scalar|oT_es": 687, "scalar|odma": 1875,
    "scalar|qkv_Vev0": 687, "scalar|qkv_Vev1": 687, "scalar|qkv_kev0": 687, "scalar|qkv_kev1": 687,
    "scalar|silu0": 687, "scalar|silu1": 1970, "scalar|th0": 687, "scalar|th1": 577,
    "scalar|warm_tables": 5490, "scalar|xfT_es": 687, "scalar|xnT_es": 687, "sync|ld_g128": 608,
    "sync|ld_gnb": 650, "sync|ld_idn": 606, "sync|ld_maskp": 1202, "sync|ld_muk": 1045,
    "sync|ld_mur": 611, "sync|ld_ones1": 642, "sync|xpf": 591, "sync|xpf0": 1256,
    "tensor|g0": 1887, "tensor|g1": 1674, "tensor|gnbw2": 6099, "tensor|kT_t0": 307,
    "tensor|kT_t1": 223, "tensor|kp0": 3570, "tensor|kp1": 1725, "tensor|kv0": 1886,
    "tensor|kv1": 1727, "tensor|oT_t0": 1012, "tensor|oT_t1": 853, "tensor|qkv_K0": 426,
    "tensor|qkv_K1": 426, "tensor|qkv_V0": 584, "tensor|qkv_V1": 427, "tensor|qkv_k0": 308,
    "tensor|qkv_k1": 308, "tensor|qkv_q0": 223, "tensor|qkv_q1": 224, "tensor|r0": 1674,
    "tensor|r1": 1727, "tensor|ret0": 626, "tensor|ret1": 1010, "tensor|sc0": 223,
    "tensor|sc1": 308, "tensor|sc2": 223, "tensor|sc3": 307, "tensor|stmm": 639,
    "tensor|w2a": 3618, "tensor|w2b": 2150, "tensor|xfT_t0": 586, "tensor|xfT_t1": 427,
    "tensor|xnT_t0": 585, "tensor|xnT_t1": 427, "vector|dm0": 603, "vector|dm1": 691,
    "vector|dm2": 681, "vector|dm3": 671, "vector|dmix": 693, "vector|f0": 686,
    "vector|f1": 687, "vector|ge": 174, "vector|gm": 81, "vector|gm2": 143,
    "vector|gnbw2_ev0": 678, "vector|gnbw2_ev1": 593, "vector|gnm": 174, "vector|gnv": 2106,
    "vector|grv": 253, "vector|gs0": 601, "vector|gs1": 692, "vector|gs2": 1220,
    "vector|gv": 176, "vector|gve": 175, "vector|ln1_ex2": 61, "vector|ln1_m2": 219,
    "vector|ln1_mean": 153, "vector|ln1_negmr": 215, "vector|ln1_rvar": 70, "vector|ln1_sq": 570,
    "vector|ln1_sum": 568, "vector|ln1_veps": 154, "vector|ln2_cq": 159, "vector|ln2_cs": 159,
    "vector|ln2_ex2": 154, "vector|ln2_m2": 219, "vector|ln2_mean": 154, "vector|ln2_negmr": 214,
    "vector|ln2_rvar": 164, "vector|ln2_sumA": 306, "vector|ln2_sumB": 350, "vector|ln2_veps": 154,
    "vector|ln3_ex2": 154, "vector|ln3_m2": 219, "vector|ln3_mean": 153, "vector|ln3_negmr": 214,
    "vector|ln3_rvar": 164, "vector|ln3_sq": 617, "vector|ln3_sum": 568, "vector|ln3_veps": 154,
    "vector|ln4_ex2": 153, "vector|ln4_m2": 126, "vector|ln4_mean": 60, "vector|ln4_negmr": 214,
    "vector|ln4_rvar": 164, "vector|ln4_sq": 570, "vector|ln4_sum": 568, "vector|ln4_veps": 154,
    "vector|mk": 2100, "vector|mr": 2097, "vector|nsc": 691, "vector|oT_ev": 691,
    "vector|out": 1224, "vector|qkv_Kev0": 691, "vector|qkv_Kev1": 680, "vector|qkv_qtE0": 601,
    "vector|qkv_qtE1": 692, "vector|qkv_qtO0": 598, "vector|qkv_qtO1": 596, "vector|sacc": 687,
    "vector|warm_dve": 1014, "vector|xfT_ev": 691, "vector|xnT_ev": 692, "vector|xp": 915,
    "vector|xres": 1225,
}


F32 = mybir.dt.float32
BF16 = mybir.dt.bfloat16
AF = mybir.ActivationFunctionType
OP = mybir.AluOpType
AX = mybir.AxisListType
BF = ml_dtypes.bfloat16

HID, NH, HD = 1024, 16, 64
NB = HID // 128
NP = NH // 2
CH = 128
EPS = 1e-5

N_CORES = 8
SPAN = 2048
N_MAIN = SPAN // CH
N_STATE = 47
PRE = (N_STATE + 1) * CH


def gammas_f32():
    lin = np.linspace(np.log(1.0 / 512.0), np.log(1.0 / 32.0), NH)
    return (1.0 - np.exp(lin)).astype(np.float32)



ENGINES = ("tensor", "vector", "scalar", "sync", "gpsimd")
ESEM = {"tensor": "sT", "vector": "sV", "scalar": "sS", "gpsimd": "sG"}
QSEM = {"sync": "dD", "gpsimd": "dG", "scalar": "dA"}


class Step:
    __slots__ = ("eng", "fn", "reads", "writes", "dma", "idx",
                 "deps", "inc_val", "needs_inc", "tag", "small")

    def __init__(self, eng, fn, reads, writes, dma, idx, tag, small):
        self.eng, self.fn = eng, fn
        self.reads, self.writes = list(reads), list(writes)
        self.dma = dma
        self.idx = idx
        self.deps = []
        self.inc_val = None
        self.needs_inc = False
        self.tag = tag
        self.small = small


class Res:
    __slots__ = ("name", "last_write", "readers", "exclusive")

    def __init__(self, name, exclusive=False):
        self.name = name
        self.last_write = None
        self.readers = []
        self.exclusive = exclusive


class Plan:
    def __init__(self, safe=False):
        self.steps = []
        self.safe = safe

    def do(self, eng, fn, reads=(), writes=(), dma=False, tag="", small=None):
        if small is None:
            small = False
        st = Step(eng, fn, reads, writes, dma, len(self.steps), tag, small)
        dep = set()
        for r in st.reads:
            if r.last_write is not None:
                dep.add(r.last_write)
            if r.exclusive:
                for rd in r.readers:
                    if rd.eng != eng:
                        dep.add(rd)
        for w in st.writes:
            if w.last_write is not None:
                dep.add(w.last_write)
            for rd in w.readers:
                dep.add(rd)
        st.deps = [d for d in dep if d is not st]
        for r in st.reads:
            r.readers.append(st)
        for w in st.writes:
            w.last_write = st
            w.readers = []
        self.steps.append(st)
        return st

    def schedule(self):
        import heapq

        def cost(st):
            t, e = st.tag, st.eng
            if st.dma:
                return 2000
            mv = MEASURED_COST.get(e + "|" + _re.sub(r"^c\d+_", "", t))
            if mv is not None:
                return mv
            if e == "tensor":
                if "_t0" in t or "_t1" in t:
                    return 600
                if "_q0" in t or "_q1" in t or "_k0" in t or "_k1" in t:
                    return 500
                if "_K0" in t or "_K1" in t or "_V0" in t or "_V1" in t:
                    return 500
                if "_sc" in t:
                    return 350
                if "_ret" in t:
                    return 900
                if "_stmm" in t:
                    return 900
                if ("_g0" in t or "_g1" in t or "_w2" in t or "_r0" in t
                        or "_r1" in t or "_kp" in t or "_kv" in t):
                    return 1900
                return 400
            if e == "scalar":
                if "_rstd" in t:
                    return 500
                if "_ln" in t or "_silu" in t or "_th" in t or "_kr" in t                         or "_at2" in t or "_gsq" in t or "_Vev" in t                         or "_kev" in t or "_es" in t or "gns" in t:
                    return 900
                return 400
            if e == "gpsimd":
                return 2600
            if "_sum" in t or "_sq" in t:
                return 1200
            if ("_qtE" in t or "_qtO" in t or "_Kev" in t or "_dm" in t
                    or "_ev" in t or "_silu" in t or "_f0" in t or "_f1" in t
                    or "_nsc" in t or "_outO" in t or "_xres" in t
                    or "_out" in t or "_ksq" in t):
                return 800
            if "_sdec" in t or "_sacc" in t or "_sbf" in t:
                return 650
            if "_mr" in t or "_mk" in t or "_xp" in t or "_gnv" in t:
                return 1500
            if "_gs" in t:
                return 700
            return 220

        succ = {st: [] for st in self.steps}
        indeg = {st: 0 for st in self.steps}
        for st in self.steps:
            for d in st.deps:
                succ[d].append(st)
                indeg[st] += 1
        prio = {}
        for st in reversed(self.steps):
            prio[st] = cost(st) + max((prio[s] for s in succ[st]), default=0)
        eng_avail = {e: 0.0 for e in ENGINES}
        finish = {}
        ready = []
        for st in self.steps:
            if indeg[st] == 0:
                heapq.heappush(ready, (-prio[st], st.idx, st))
        order = []
        while ready:
            _, _, st = heapq.heappop(ready)
            start = eng_avail[st.eng]
            for d in st.deps:
                hop = 300 if (d.eng != st.eng or d.dma) else 0
                start = max(start, finish[d] + hop)
            f = start + cost(st)
            finish[st] = f
            eng_avail[st.eng] = f
            order.append(st)
            for s in succ[st]:
                indeg[s] -= 1
                if indeg[s] == 0:
                    heapq.heappush(ready, (-prio[s], s.idx, s))
        assert len(order) == len(self.steps)
        self.steps = order
        for k, st in enumerate(self.steps):
            st.idx = k

    def resolve(self):
        def needs_edge(d, st):
            return (d.dma or d.eng != st.eng or self.safe
                    or (d.eng == "vector" and d.small))

        for st in self.steps:
            for d in st.deps:
                if needs_edge(d, st):
                    d.needs_inc = True
        cnt = {}
        for st in self.steps:
            if st.dma:
                sem = QSEM[st.eng]
                cnt[sem] = cnt.get(sem, 0) + 16
                st.inc_val = cnt[sem]
                st.needs_inc = True
            elif st.needs_inc:
                sem = ESEM[st.eng]
                cnt[sem] = cnt.get(sem, 0) + 1
                st.inc_val = cnt[sem]
        marks = {e: {} for e in ENGINES}
        self.waits = {}
        for st in self.steps:
            w = {}
            for d in st.deps:
                if not needs_edge(d, st):
                    continue
                sem = QSEM[d.eng] if d.dma else ESEM[d.eng]
                assert d.inc_val is not None
                if marks[st.eng].get(sem, 0) < d.inc_val:
                    w[sem] = max(w.get(sem, 0), d.inc_val)
            for sem, v in w.items():
                marks[st.eng][sem] = v
            self.waits[st.idx] = w
        return cnt

    def run(self, eng_name, eng, sems, nc=None, annotate=False):
        for st in self.steps:
            if st.eng != eng_name:
                continue
            for sem, v in self.waits[st.idx].items():
                eng.wait_ge(sems[sem], v)
            last = st.fn(eng)
            if st.needs_inc:
                sem = QSEM[st.eng] if st.dma else ESEM[st.eng]
                assert last is not None, f"step {st.tag} returned no instruction"
                last.then_inc(sems[sem], 16 if st.dma else 1)
            if last is not None and annotate:
                try:
                    last.annotate(st.tag)
                except Exception:
                    pass



def make_host_constants():
    g = gammas_f32().astype(np.float64)
    t = np.arange(CH, dtype=np.float64)
    dt = np.zeros((CH, NH * CH))
    gq = np.zeros((CH, NP * CH))
    lam = np.zeros((CH, NP * CH))
    g128 = np.zeros((CH, NP * HD))
    causal = (t[:, None] <= t[None, :]).astype(np.float64)
    for h in range(NH):
        gh = g[h]
        dt[:, h * CH:(h + 1) * CH] = (gh ** (-(t + 1.0)))[:, None] * causal
        p, j = h // 2, h % 2
        gq[j * 64:(j + 1) * 64, p * CH:(p + 1) * CH] = (gh ** (t + 1.0))[None, :]
        lam[:, p * CH + j * 64: p * CH + (j + 1) * 64] = (gh ** (127.0 - t))[:, None]
        g128[j * 64:(j + 1) * 64, p * HD:(p + 1) * HD] = gh ** 128.0
    gqE = gq.copy(); gqE[64:128, :] = 0.0
    gqO = gq.copy(); gqO[0:64, :] = 0.0
    return (dt.astype(np.float32), gqE.astype(np.float32),
            gqO.astype(np.float32), lam.astype(np.float32),
            g128.astype(np.float32))


def band_layout(w):
    n = w.shape[1]
    return np.ascontiguousarray(
        w.reshape(NB, 128, n).transpose(1, 0, 2).reshape(128, NB * n))


def blkdiag_layout(wh):
    out = np.zeros((128, NP * 128), dtype=wh.dtype)
    for q in range(NP):
        out[0:64, q * 128:q * 128 + 64] = wh[2 * q]
        out[64:128, q * 128 + 64:q * 128 + 128] = wh[2 * q + 1]
    return out


def colwise(v):
    return np.ascontiguousarray(v.reshape(NB, 128).T)



def build_program(n_state=N_STATE, n_main=N_MAIN, safe=False, debug_dump=(), annotate=False):
    nc = bass.Bass("TRN2", target_bir_lowering=False, debug=False,
                   detect_race_conditions=False)
    pre_len = (n_state + 1) * CH
    span = n_main * CH
    NCH = n_state + 1 + n_main

    d = {}
    d['x_pre'] = nc.dram_tensor("x_pre", [pre_len, HID], F32, kind="ExternalInput")
    d['x_main'] = nc.dram_tensor("x_main", [span, HID], F32, kind="ExternalInput")
    d['maskp'] = nc.dram_tensor("maskp", [128, 1], F32, kind="ExternalInput")
    for nm in ("wq_bd", "wk_bd", "wv_bd"):
        d[nm] = nc.dram_tensor(nm, [128, NP * 128], F32, kind="ExternalInput")
    for nm in ("w1", "w2", "wrT", "wkT", "wvT"):
        d[nm] = nc.dram_tensor(nm, [128, NB * HID], F32, kind="ExternalInput")
    d['dt'] = nc.dram_tensor("dt", [128, NH * CH], F32, kind="ExternalInput")
    d['gqE'] = nc.dram_tensor("gqE", [128, NP * CH], F32, kind="ExternalInput")
    d['gqO'] = nc.dram_tensor("gqO", [128, NP * CH], F32, kind="ExternalInput")
    d['lam'] = nc.dram_tensor("lam", [128, NP * CH], F32, kind="ExternalInput")
    d['g128'] = nc.dram_tensor("g128", [128, NP * HD], F32, kind="ExternalInput")
    d['gnsc'] = nc.dram_tensor("gnsc", [128, HID], F32, kind="ExternalInput")
    d['gnb'] = nc.dram_tensor("gnb", [128, NB], F32, kind="ExternalInput")
    d['mur'] = nc.dram_tensor("mur", [128, NB], F32, kind="ExternalInput")
    d['muk'] = nc.dram_tensor("muk", [128, NB], F32, kind="ExternalInput")
    d['idn'] = nc.dram_tensor("idn", [128, 128], BF16, kind="ExternalInput")
    d['ones1'] = nc.dram_tensor("ones1", [1, 128], BF16, kind="ExternalInput")
    d['out'] = nc.dram_tensor("out", [span, HID], F32, kind="ExternalOutput")

    S = {}

    def sb(name, shape, dtype):
        S[name] = nc.alloc_sbuf_tensor("s_" + name, shape, dtype)
        return S[name]

    sb("wq", [128, NP * 128], BF16); sb("wk", [128, NP * 128], BF16)
    sb("wv", [128, NP * 128], BF16)
    sb("w1", [128, NB * HID], BF16); sb("w2", [128, NB * HID], BF16)
    sb("wrT", [128, NB * HID], BF16); sb("wkT", [128, NB * HID], BF16)
    sb("wvT", [128, NB * HID], BF16)
    sb("dt", [128, NH * CH], BF16); sb("gqE", [128, NP * CH], BF16)
    sb("gqO", [128, NP * CH], BF16)
    sb("lam", [128, NP * CH], BF16); sb("g128", [128, NP * HD], F32)
    sb("gnsc", [128, HID], BF16); sb("gnb", [128, NB], F32)
    sb("gnbb", [128, NB], BF16)
    sb("mur", [128, NB], F32); sb("muk", [128, NB], F32)
    sb("murn", [128, NB], F32); sb("mukn", [128, NB], F32)
    sb("nmaskp", [128, 1], F32)
    sb("idn", [128, 128], BF16); sb("ones1", [1, 128], BF16)
    sb("maskp", [128, 1], F32)
    sb("gnbw2", [1, HID], BF16)
    sb("sst", [128, NP * HD], F32); sb("sstd", [128, NP * HD], F32)
    sb("sstb", [128, NP * HD], BF16)
    for k in range(3):
        sb(f"x{k}", [128, HID], F32)
    for k in range(2):
        sb(f"xnT{k}", [128, NB * 128], BF16)
        sb(f"xfT{k}", [128, NB * 128], BF16)
        sb(f"xres{k}", [128, HID], F32)
        sb(f"outb{k}", [128, HID], F32)
    sb("qtE", [128, NP * 128], BF16); sb("qtO", [128, NP * 128], BF16)
    sb("kt", [128, NP * 128], BF16)
    for k in range(2):
        sb(f"kk{k}", [128, NP * 128], BF16); sb(f"vv{k}", [128, NP * 128], BF16)
        sb(f"tanh{k}", [128, HID], BF16); sb(f"krelu{k}", [128, HID], BF16)
    sb("xrT", [128, NB * 128], BF16); sb("xkT", [128, NB * 128], BF16)
    sb("at4", [128, HID], F32)
    sb("ds", [128, NH * CH], BF16)
    sb("norm", [128, HID], BF16)
    sb("nsc", [128, HID], BF16)
    sb("silu", [128, HID], BF16)
    sb("outT", [128, NB * 128], BF16)
    sb("at2", [128, HID], F32)
    sb("xf", [128, HID], BF16)
    sb("xn0", [128, HID], BF16); sb("xn1", [128, HID], BF16)
    sb("dmix", [128, NB * 128], BF16)
    sb("fbuf", [128, HID], F32)
    sb("kactT", [128, NB * 128], BF16)
    sb("bnst", [128, 12], F32)
    for ts in ("1", "2"):
        sb("bnst" + ts, [128, 12], F32)
        sb("mv" + ts, [128, 2], F32); sb("sag" + ts, [128, 4], F32)
        sb("g4a" + ts, [128, 4], F32)
        sb("veps" + ts, [128, 1], F32); sb("rvar" + ts, [128, 1], F32)
        sb("rstd" + ts, [128, 1], F32); sb("negmr" + ts, [128, 1], F32)
    sb("g4a", [128, 4], F32)
    sb("sag", [128, 4], F32)
    sb("mv", [128, 2], F32)
    sb("veps", [128, 1], F32); sb("rvar", [128, 1], F32)
    sb("rstd", [128, 1], F32); sb("negmr", [128, 1], F32)
    sb("dump", [128, HID], BF16)
    sb("g_sum16", [128, 16], F32); sb("g_sq16", [128, 16], F32)
    sb("g_mean16", [128, 16], F32); sb("g_ex216", [128, 16], F32)
    sb("g_m2", [128, 16], F32); sb("g_var16", [128, 16], F32)
    sb("g_rv16", [128, 16], F32); sb("g_rstd16", [128, 16], F32)
    sb("g_negmr16", [128, 16], F32)

    P = [nc.alloc_psum_tensor(f"p{i}", [128, 512], F32) for i in range(8)]

    R = {}

    def res(name):
        if name not in R:
            R[name] = Res(name)
        return R[name]

    for nm in S:
        res(nm)
    for i in range(8):
        R[f"P{i}"] = Res(f"P{i}", exclusive=True)

    pl = Plan(safe=safe)

    for nm in ("g128", "gnb", "mur", "muk",
               "maskp", "idn", "ones1"):
        pl.do("sync", (lambda e, nm=nm: e.dma_start(S[nm][:], d[nm][:])),
              writes=[res(nm)], dma=True, tag=f"ld_{nm}")
    for nm, dst in (("lam", "lam"), ("wk_bd", "wk"), ("wv_bd", "wv"),
                    ("dt", "dt"), ("gqE", "gqE"), ("gqO", "gqO"), ("gnsc", "gnsc"), ("wq_bd", "wq"),
                    ("w1", "w1"), ("w2", "w2"), ("wrT", "wrT"),
                    ("wkT", "wkT"), ("wvT", "wvT")):
        pl.do("gpsimd", (lambda e, nm=nm, dst=dst: e.dma_start(S[dst][:], d[nm][:])),
              writes=[res(dst)], dma=True, tag=f"ld_{nm}")
    def warm_tables(e):
        last = None
        for fn in (AF.Silu, AF.Tanh, AF.Sqrt):
            for _ in range(12):
                last = e.activation(S["g4a"][:, 0:1], S["maskp"][:], fn)
        for _ in range(4):
            last = e.activation(S["g4a"][:, 0:1], S["maskp"][:], AF.Square,
                                accum_out=S["g4a"][:, 1:2])
        return last
    pl.do("scalar", warm_tables, reads=[res("maskp")], writes=[res("g4a")],
          tag="warm_tables")

    def warm_dve_accum(e):
        last = None
        for _ in range(4):
            last = e.tensor_scalar(out=S["sag"][:, 0:1], in0=S["maskp"][:],
                                   scalar1=1.0, scalar2=0.0, op0=OP.mult,
                                   op1=OP.add, accum_out=S["sag"][:, 1:2])
            last = e.scalar_tensor_tensor(out=S["sag"][:, 2:3], in0=S["maskp"][:],
                                          scalar=0.0, in1=S["maskp"][:],
                                          op0=OP.add, op1=OP.mult,
                                          accum_out=S["sag"][:, 3:4])
        return last
    for nm_src, nm_dst in (("mur", "murn"), ("muk", "mukn")):
        pl.do("vector", lambda e, a=nm_src, b=nm_dst: e.tensor_scalar(
            out=S[b][:], in0=S[a][:], scalar1=1.0, scalar2=-1.0,
            op0=OP.mult, op1=OP.add),
            reads=[res(nm_src)], writes=[res(nm_dst)], tag=f"mk_{nm_dst}",
            small=True)
    pl.do("vector", lambda e: e.tensor_scalar(
        out=S["nmaskp"][:], in0=S["maskp"][:], scalar1=-1.0, scalar2=None,
        op0=OP.mult),
        reads=[res("maskp")], writes=[res("nmaskp")], tag="mk_nmaskp",
        small=True)
    pl.do("vector", warm_dve_accum, reads=[res("maskp")], writes=[res("sag")],
          tag="warm_dve", small=True)
    pl.do("vector", lambda e: e.memset(S["sst"][:], 0.0), writes=[res("sst")])
    pl.do("vector", lambda e: e.memset(S["sstb"][:], 0.0), writes=[res("sstb")])
    pl.do("vector", lambda e: e.tensor_copy(S["gnbb"][:], S["gnb"][:]),
          reads=[res("gnb")], writes=[res("gnbb")])

    def mm_gnbw2(e):
        last = None
        for half in range(2):
            for b in range(NB):
                last = e.matmul(P[6 + half][0:1, 0:512],
                                S["gnbb"][:, b:b + 1],
                                S["w2"][:, b * HID + half * 512: b * HID + (half + 1) * 512],
                                start=(b == 0), stop=(b == NB - 1))
        return last
    pl.do("tensor", mm_gnbw2, reads=[res("gnbb"), res("w2")],
          writes=[res("P6"), res("P7")], tag="gnbw2")
    pl.do("vector", lambda e: e.tensor_copy(S["gnbw2"][0:1, 0:512], P[6][0:1, 0:512]),
          reads=[res("P6")], writes=[res("gnbw2")], tag="gnbw2_ev0")
    pl.do("vector", lambda e: e.tensor_copy(S["gnbw2"][0:1, 512:1024], P[7][0:1, 0:512]),
          reads=[res("P7")], writes=[res("gnbw2")], tag="gnbw2_ev1")

    def x_src(i):
        if i < n_state + 1:
            return d['x_pre'][i * CH:(i + 1) * CH, :]
        j = i - (n_state + 1)
        return d['x_main'][j * CH:(j + 1) * CH, :]


    def ln_stats(xres_name, src_ap_fn, from_psum2=None, tag="", ts="",
                 sq_dump=None, eps=EPS):
        MV, VE = "mv" + ts, "veps" + ts
        RV, RS, NM = "rvar" + ts, "rstd" + ts, "negmr" + ts
        dump = S["dump"][:]
        if from_psum2 is None:
            pl.do("vector", lambda e: e.tensor_scalar(
                out=dump, in0=src_ap_fn(e), scalar1=1.0, scalar2=0.0,
                op0=OP.mult, op1=OP.add, accum_out=S[MV][:, 0:1]),
                reads=[res(xres_name)], writes=[res(MV)],
                tag=f"{tag}_sum", small=True)
            if False:
                pass
            else:
                pl.do("vector", lambda e: e.scalar_tensor_tensor(
                    out=dump, in0=src_ap_fn(e), scalar=0.0, in1=src_ap_fn(e),
                    op0=OP.add, op1=OP.mult, accum_out=S[MV][:, 1:2]),
                    reads=[res(xres_name)], writes=[res(MV)],
                    tag=f"{tag}_sq", small=True)
        else:
            pa, pb = from_psum2
            pl.do("vector", lambda e: e.tensor_scalar(
                out=dump[:, 0:512], in0=P[pa][:, :], scalar1=1.0, scalar2=0.0,
                op0=OP.mult, op1=OP.add, accum_out=S["sag" + ts][:, 0:1]),
                reads=[res(f"P{pa}")], writes=[res("sag" + ts)],
                tag=f"{tag}_sumA", small=True)
            pl.do("vector", lambda e: e.tensor_scalar(
                out=dump[:, 512:1024], in0=P[pb][:, :], scalar1=1.0, scalar2=0.0,
                op0=OP.mult, op1=OP.add, accum_out=S["sag" + ts][:, 1:2]),
                reads=[res(f"P{pb}")], writes=[res("sag" + ts)],
                tag=f"{tag}_sumB", small=True)
            pl.do("scalar", lambda e: e.activation(
                S["dump"][:, 0:512], P[pa][:, :], AF.Square,
                accum_out=S["sag" + ts][:, 2:3]),
                reads=[res(f"P{pa}")], writes=[res("sag" + ts)],
                tag=f"{tag}_sqA")
            pl.do("scalar", lambda e: e.activation(
                S["dump"][:, 512:1024], P[pb][:, :], AF.Square,
                accum_out=S["sag" + ts][:, 3:4]),
                reads=[res(f"P{pb}")], writes=[res("sag" + ts)],
                tag=f"{tag}_sqB")
            pl.do("vector", lambda e: e.tensor_tensor(
                out=S[MV][:, 0:1], in0=S["sag" + ts][:, 0:1],
                in1=S["sag" + ts][:, 1:2], op=OP.add),
                reads=[res("sag" + ts)], writes=[res(MV)],
                tag=f"{tag}_cs", small=True)
            pl.do("vector", lambda e: e.tensor_tensor(
                out=S[MV][:, 1:2], in0=S["sag" + ts][:, 2:3],
                in1=S["sag" + ts][:, 3:4], op=OP.add),
                reads=[res("sag" + ts)], writes=[res(MV)],
                tag=f"{tag}_cq", small=True)
        pl.do("vector", lambda e: e.tensor_scalar(
            out=S[MV][:, 0:1].tensor[:, 0:1] if False else S["g4a" + ts][:, 0:1],
            in0=S[MV][:, 0:1], scalar1=1.0 / 1024, scalar2=None, op0=OP.mult),
            reads=[res(MV)], writes=[res("g4a" + ts)], tag=f"{tag}_mean",
            small=True)
        pl.do("vector", lambda e: e.tensor_scalar(
            out=S["g4a" + ts][:, 1:2], in0=S[MV][:, 1:2], scalar1=1.0 / 1024,
            scalar2=None, op0=OP.mult),
            reads=[res(MV)], writes=[res("g4a" + ts)], tag=f"{tag}_ex2",
            small=True)
        pl.do("vector", lambda e: e.scalar_tensor_tensor(
            out=S["g4a" + ts][:, 2:3], in0=S["g4a" + ts][:, 0:1],
            scalar=S["g4a" + ts][:, 0:1], in1=S["g4a" + ts][:, 1:2],
            op0=OP.mult, op1=OP.subtract),
            reads=[res("g4a" + ts)], writes=[res("g4a" + ts)], tag=f"{tag}_m2",
            small=True)
        pl.do("vector", lambda e, eps=eps: e.tensor_scalar(
            out=S[VE][:], in0=S["g4a" + ts][:, 2:3], scalar1=-1.0, scalar2=eps,
            op0=OP.mult, op1=OP.add),
            reads=[res("g4a" + ts)], writes=[res(VE)], tag=f"{tag}_veps",
            small=True)
        pl.do("vector", lambda e: e.reciprocal(S[RV][:], S[VE][:]),
              reads=[res(VE)], writes=[res(RV)], tag=f"{tag}_rvar", small=True)
        pl.do("scalar", lambda e: e.activation(S[RS][:], S[RV][:], AF.Sqrt),
              reads=[res(RV)], writes=[res(RS)], tag=f"{tag}_rstd")
        pl.do("vector", lambda e: e.tensor_scalar(
            out=S[NM][:], in0=S["g4a" + ts][:, 0:1], scalar1=S[RS][:],
            scalar2=-1.0, op0=OP.mult, op1=OP.mult),
            reads=[res("g4a" + ts), res(RS)], writes=[res(NM)],
            tag=f"{tag}_negmr", small=True)

    def transpose_mm(src_name, tag, tp=(0, 1), start=True, stop=True):
        pa, pb = tp

        def mmt(e, lo, pi, start, stop):
            last = None
            for b in range(lo, lo + 4):
                last = e.matmul(P[pi][:, (b % 4) * 128:(b % 4) * 128 + 128],
                                S[src_name][:, b * 128:(b + 1) * 128],
                                S["idn"][:], start=start, stop=stop)
            return last
        pl.do("tensor", lambda e: mmt(e, 0, pa, start, stop),
              reads=[res(src_name), res("idn")],
              writes=[res(f"P{pa}")], tag=f"{tag}_t0")
        pl.do("tensor", lambda e: mmt(e, 4, pb, start, stop),
              reads=[res(src_name), res("idn")],
              writes=[res(f"P{pb}")], tag=f"{tag}_t1")

    def transpose_evac(dst_name, tag, tp=(0, 1), func=None):
        pa, pb = tp
        if func is None:
            pl.do("vector", lambda e: e.tensor_copy(S[dst_name][:, 0:512],
                                                    P[pa][:, :]),
                  reads=[res(f"P{pa}")], writes=[res(dst_name)], tag=f"{tag}_ev")
        else:
            pl.do("scalar", lambda e: e.activation(S[dst_name][:, 0:512],
                                                   P[pa][:, :], func),
                  reads=[res(f"P{pa}")], writes=[res(dst_name)], tag=f"{tag}_ev")
        pl.do("scalar", lambda e: e.activation(S[dst_name][:, 512:1024], P[pb][:, :],
                                               func or AF.Identity),
              reads=[res(f"P{pb}")], writes=[res(dst_name)], tag=f"{tag}_es")

    def transpose8(src_name, dst_name, tag, tp=(0, 1)):
        transpose_mm(src_name, tag, tp)
        transpose_evac(dst_name, tag, tp)

    def qkv_proj(xnT_name, want_qt_kt, tag, kkn, vvn, bk=(2, 3), bv=(4, 5)):
        if want_qt_kt:
            def mmw(e, w_name, pb, lo, hi):
                last = None
                for p in range(lo, hi):
                    last = e.matmul(P[pb][:, (p - lo) * 128:(p - lo) * 128 + 128],
                                    S[w_name][:, p * 128:(p + 1) * 128],
                                    S[xnT_name][:, p * 128:(p + 1) * 128],
                                    start=True, stop=True)
                return last
            pl.do("tensor", lambda e: mmw(e, "wq", 2, 0, 4),
                  reads=[res("wq"), res(xnT_name)], writes=[res("P2")], tag=f"{tag}_q0")
            pl.do("tensor", lambda e: mmw(e, "wq", 3, 4, 8),
                  reads=[res("wq"), res(xnT_name)], writes=[res("P3")], tag=f"{tag}_q1")
            pl.do("tensor", lambda e: mmw(e, "wk", 4, 0, 4),
                  reads=[res("wk"), res(xnT_name)], writes=[res("P4")], tag=f"{tag}_k0")
            pl.do("tensor", lambda e: mmw(e, "wk", 5, 4, 8),
                  reads=[res("wk"), res(xnT_name)], writes=[res("P5")], tag=f"{tag}_k1")
            for half, pb in ((0, 2), (1, 3)):
                for which in ("qtE", "qtO"):
                    pl.do("vector", lambda e, half=half, pb=pb, which=which:
                          e.tensor_tensor(
                        out=S[which][:, half * 512:(half + 1) * 512],
                        in0=P[pb][:, :],
                        in1=S["gq" + which[-1]][:, half * 512:(half + 1) * 512],
                        op=OP.mult),
                        reads=[res(f"P{pb}"), res("gq" + which[-1])],
                        writes=[res(which)], tag=f"{tag}_{which}{half}")
            for half, pb in ((0, 4), (1, 5)):
                pl.do("scalar", lambda e, half=half, pb=pb: e.activation(
                    S["kt"][:, half * 512:(half + 1) * 512], P[pb][:, :], AF.Identity),
                    reads=[res(f"P{pb}")], writes=[res("kt")], tag=f"{tag}_kev{half}")

        def mmKV(e, w_name, pb, lo, hi):
            last = None
            for p in range(lo, hi):
                last = e.matmul(P[pb][:, (p - lo) * 128:(p - lo) * 128 + 128],
                                S[xnT_name][:, p * 128:(p + 1) * 128],
                                S[w_name][:, p * 128:(p + 1) * 128],
                                start=True, stop=True)
            return last
        pl.do("tensor", lambda e: mmKV(e, "wk", bk[0], 0, 4),
              reads=[res("wk"), res(xnT_name)], writes=[res(f"P{bk[0]}")], tag=f"{tag}_K0")
        pl.do("tensor", lambda e: mmKV(e, "wk", bk[1], 4, 8),
              reads=[res("wk"), res(xnT_name)], writes=[res(f"P{bk[1]}")], tag=f"{tag}_K1")
        pl.do("tensor", lambda e: mmKV(e, "wv", bv[0], 0, 4),
              reads=[res("wv"), res(xnT_name)], writes=[res(f"P{bv[0]}")], tag=f"{tag}_V0")
        pl.do("tensor", lambda e: mmKV(e, "wv", bv[1], 4, 8),
              reads=[res("wv"), res(xnT_name)], writes=[res(f"P{bv[1]}")], tag=f"{tag}_V1")
        for half, pb in ((0, bk[0]), (1, bk[1])):
            pl.do("vector", lambda e, half=half, pb=pb: e.tensor_tensor(
                out=S[kkn][:, half * 512:(half + 1) * 512], in0=P[pb][:, :],
                in1=S["lam"][:, half * 512:(half + 1) * 512], op=OP.mult),
                reads=[res(f"P{pb}"), res("lam")], writes=[res(kkn)],
                tag=f"{tag}_Kev{half}")
        for half, pb in ((0, bv[0]), (1, bv[1])):
            pl.do("scalar", lambda e, half=half, pb=pb: e.activation(
                S[vvn][:, half * 512:(half + 1) * 512], P[pb][:, :], AF.Identity),
                reads=[res(f"P{pb}")], writes=[res(vvn)], tag=f"{tag}_Vev{half}")

    def state_update(tag, kkn, vvn, st_bank=0, want_sbf=True):
        def mmst(e):
            last = None
            for p in range(NP):
                for j in range(2):
                    last = e.matmul(
                        P[st_bank][j * 64:(j + 1) * 64, p * 64:(p + 1) * 64],
                        S[kkn][:, p * 128 + j * 64: p * 128 + (j + 1) * 64],
                        S[vvn][:, p * 128 + j * 64: p * 128 + (j + 1) * 64],
                        start=True, stop=True)
            return last
        pl.do("tensor", mmst, reads=[res(kkn), res(vvn)], writes=[res(f"P{st_bank}")],
              tag=f"{tag}_stmm")
        pl.do("gpsimd", lambda e: e.tensor_tensor(
            out=S["sstd"][:], in0=S["sst"][:], in1=S["g128"][:], op=OP.mult),
            reads=[res("sst"), res("g128")], writes=[res("sstd")], tag=f"{tag}_sdec")
        pl.do("vector", lambda e: e.tensor_tensor(
            out=S["sst"][:], in0=S["sstd"][:], in1=P[st_bank][:, 0:NP * HD], op=OP.add),
            reads=[res("sstd"), res(f"P{st_bank}")], writes=[res("sst")],
            tag=f"{tag}_sacc")
        if want_sbf:
            pl.do("gpsimd", lambda e: e.tensor_copy(S["sstb"][:], S["sst"][:]),
                  reads=[res("sst")], writes=[res("sstb")], tag=f"{tag}_sbf")


    def emit_ln1(i):
        xb = f"x{i % 3}"
        xnm = f"xn{i % 2}"
        ln_stats(xb, lambda e, xb=xb: S[xb][:], tag=f"c{i}_ln1", ts="1",
                 sq_dump=xnm)
        pl.do("scalar", lambda e, xb=xb, xnm=xnm: e.activation(
            S[xnm][:], S[xb][:], AF.Identity, bias=S["negmr1"][:],
            scale=S["rstd1"][:]),
            reads=[res(xb), res("rstd1"), res("negmr1")], writes=[res(xnm)],
            tag=f"c{i}_ln1n")

    def emit_rtanh(j):
        th = f"tanh{j % 2}"
        for half, pb in ((0, 6), (1, 7)):
            pl.do("scalar", lambda e, half=half, pb=pb, th=th: e.activation(
                S[th][:, half * 512:(half + 1) * 512], P[pb][:, :], AF.Tanh,
                scale=0.5), reads=[res(f"P{pb}")], writes=[res(th)],
                tag=f"c{j}_th{half}")

    def emit_attn(i, kind):
        par = i % 2
        xb = f"x{i % 3}"
        xnm = f"xn{par}"
        xnT = f"xnT{par}"
        kkn, vvn = f"kk{par}", f"vv{par}"
        full = kind != "state"
        if full:
            tp, bk, bv, stb = (0, 1), (2, 3), (4, 5), 3
        else:
            tp = (0, 1) if par == 0 else (4, 5)
            bk = (2, 3) if par == 0 else (6, 7)
            bv = (4, 5) if par == 0 else (0, 1)
            stb = 6 if par == 0 else 2

        transpose8(xnm, xnT, f"c{i}_xnT", tp)
        qkv_proj(xnT, want_qt_kt=full, tag=f"c{i}_qkv", kkn=kkn, vvn=vvn,
                 bk=bk, bv=bv)

        if full:
            def mmgate(e, pb, half):
                last = None
                for b in range(NB):
                    last = e.matmul(
                        P[pb][:, :],
                        S[xnT][:, b * 128:(b + 1) * 128],
                        S["w1"][:, b * HID + half * 512: b * HID + (half + 1) * 512],
                        start=(b == 0), stop=(b == NB - 1))
                return last
            pl.do("tensor", lambda e: mmgate(e, 4, 0), reads=[res(xnT), res("w1")],
                  writes=[res("P4")], tag=f"c{i}_g0")
            pl.do("tensor", lambda e: mmgate(e, 5, 1), reads=[res(xnT), res("w1")],
                  writes=[res("P5")], tag=f"c{i}_g1")
            for half, pb in ((0, 4), (1, 5)):
                pl.do("scalar", lambda e, half=half, pb=pb: e.activation(
                    S["silu"][:, half * 512:(half + 1) * 512], P[pb][:, :], AF.Silu),
                    reads=[res(f"P{pb}")], writes=[res("silu")], tag=f"c{i}_silu{half}")

            for rnd in range(4):
                pb = (0, 1, 0, 1)[rnd]
                heads = range(rnd * 4, rnd * 4 + 4)

                def mmsc(e, pb=pb, heads=tuple(heads)):
                    last = None
                    for n, h in enumerate(heads):
                        p, j = h // 2, h % 2
                        qn = "qtE" if j == 0 else "qtO"
                        last = e.matmul(
                            P[pb][:, n * 128:(n + 1) * 128],
                            S["kt"][:, p * 128:(p + 1) * 128],
                            S[qn][:, p * 128:(p + 1) * 128],
                            start=True, stop=True)
                    return last
                pl.do("tensor", mmsc, reads=[res("kt"), res("qtE"), res("qtO")],
                      writes=[res(f"P{pb}")], tag=f"c{i}_sc{rnd}")
                pl.do("vector", lambda e, rnd=rnd, pb=pb: e.tensor_tensor(
                    out=S["ds"][:, rnd * 512:(rnd + 1) * 512], in0=P[pb][:, :],
                    in1=S["dt"][:, rnd * 512:(rnd + 1) * 512], op=OP.mult),
                    reads=[res(f"P{pb}"), res("dt")], writes=[res("ds")],
                    tag=f"c{i}_dm{rnd}")

            def mmret(e, pb, lo):
                last = None
                for h in range(lo, lo + 8):
                    p, j = h // 2, h % 2
                    qn = "qtE" if j == 0 else "qtO"
                    c = (h - lo) * 64
                    last = e.matmul(P[pb][:, c:c + 64],
                                    S["ds"][:, h * 128:(h + 1) * 128],
                                    S[vvn][:, p * 128 + j * 64: p * 128 + (j + 1) * 64],
                                    start=True, stop=False)
                    last = e.matmul(P[pb][:, c:c + 64],
                                    S[qn][:, p * 128:(p + 1) * 128],
                                    S["sstb"][:, p * 64:(p + 1) * 64],
                                    start=False, stop=True)
                return last
            pl.do("tensor", lambda e: mmret(e, 0, 0),
                  reads=[res("ds"), res(vvn), res("qtE"), res("qtO"), res("sstb")],
                  writes=[res("P0")], tag=f"c{i}_ret0")
            pl.do("tensor", lambda e: mmret(e, 1, 8),
                  reads=[res("ds"), res(vvn), res("qtE"), res("qtO"), res("sstb")],
                  writes=[res("P1")], tag=f"c{i}_ret1")

        state_update(f"c{i}", kkn, vvn, stb,
                     want_sbf=(full or i + 1 >= n_state))
        if not full:
            return

        pl.do("vector", lambda e: e.reduce_sum(
            S["g_sum16"][:, 0:8], P[0][:, :].rearrange("p (h d) -> p h d", d=64),
            axis=AX.X), reads=[res("P0")], writes=[res("g_sum16")],
            tag=f"c{i}_gs0", small=True)
        pl.do("vector", lambda e: e.reduce_sum(
            S["g_sum16"][:, 8:16], P[1][:, :].rearrange("p (h d) -> p h d", d=64),
            axis=AX.X), reads=[res("P1")], writes=[res("g_sum16")],
            tag=f"c{i}_gs1", small=True)
        pl.do("scalar", lambda e: e.activation(
            S["ds"][:].bitcast(F32)[:, 0:512], P[0][:, :], AF.Square),
              reads=[res("P0")], writes=[res("ds")], tag=f"c{i}_gsq0")
        pl.do("scalar", lambda e: e.activation(
            S["ds"][:].bitcast(F32)[:, 512:1024], P[1][:, :], AF.Square),
              reads=[res("P1")], writes=[res("ds")], tag=f"c{i}_gsq1")
        pl.do("vector", lambda e: e.reduce_sum(
            S["g_sq16"][:], S["ds"][:].bitcast(F32).rearrange(
                "p (h d) -> p h d", d=64),
            axis=AX.X), reads=[res("ds")], writes=[res("g_sq16")],
            tag=f"c{i}_gs2", small=True)
        pl.do("vector", lambda e: e.tensor_scalar(
            out=S["g_mean16"][:], in0=S["g_sum16"][:], scalar1=1.0 / 64, scalar2=None,
            op0=OP.mult), reads=[res("g_sum16")], writes=[res("g_mean16")],
            tag=f"c{i}_gm", small=True)
        pl.do("vector", lambda e: e.tensor_scalar(
            out=S["g_ex216"][:], in0=S["g_sq16"][:], scalar1=1.0 / 64, scalar2=None,
            op0=OP.mult), reads=[res("g_sq16")], writes=[res("g_ex216")],
            tag=f"c{i}_ge", small=True)
        pl.do("vector", lambda e: e.tensor_tensor(
            out=S["g_m2"][:], in0=S["g_mean16"][:], in1=S["g_mean16"][:], op=OP.mult),
            reads=[res("g_mean16")], writes=[res("g_m2")], tag=f"c{i}_gm2", small=True)
        pl.do("vector", lambda e: e.tensor_tensor(
            out=S["g_var16"][:], in0=S["g_ex216"][:], in1=S["g_m2"][:],
            op=OP.subtract),
            reads=[res("g_ex216"), res("g_m2")], writes=[res("g_var16")],
            tag=f"c{i}_gv", small=True)
        pl.do("vector", lambda e: e.tensor_scalar(
            out=S["g_m2"][:], in0=S["g_var16"][:], scalar1=EPS, scalar2=None,
            op0=OP.add), reads=[res("g_var16")], writes=[res("g_m2")],
            tag=f"c{i}_gve", small=True)
        pl.do("vector", lambda e: e.reciprocal(S["g_rv16"][:], S["g_m2"][:]),
              reads=[res("g_m2")], writes=[res("g_rv16")], tag=f"c{i}_grv", small=True)
        pl.do("scalar", lambda e: e.activation(S["g_rstd16"][:], S["g_rv16"][:],
                                               AF.Sqrt),
              reads=[res("g_rv16")], writes=[res("g_rstd16")], tag=f"c{i}_grs")
        pl.do("vector", lambda e: e.scalar_tensor_tensor(
            out=S["g_negmr16"][:], in0=S["g_mean16"][:], scalar=-1.0,
            in1=S["g_rstd16"][:], op0=OP.mult, op1=OP.mult),
            reads=[res("g_mean16"), res("g_rstd16")], writes=[res("g_negmr16")],
            tag=f"c{i}_gnm", small=True)

        def gnv(e):
            last = None
            for h in range(8):
                last = e.tensor_scalar(
                    out=S["norm"][:, h * 64:(h + 1) * 64],
                    in0=P[0][:, h * 64:(h + 1) * 64],
                    scalar1=S["g_rstd16"][:, h:h + 1],
                    scalar2=S["g_negmr16"][:, h:h + 1], op0=OP.mult, op1=OP.add)
            return last
        pl.do("vector", gnv, reads=[res("P0"), res("g_rstd16"), res("g_negmr16")],
              writes=[res("norm")], tag=f"c{i}_gnv", small=True)

        def gns(e):
            last = None
            for h in range(8, 16):
                last = e.activation(
                    S["norm"][:, h * 64:(h + 1) * 64],
                    P[1][:, (h - 8) * 64:(h - 7) * 64], AF.Identity,
                    bias=S["g_negmr16"][:, h:h + 1], scale=S["g_rstd16"][:, h:h + 1])
            return last
        pl.do("scalar", gns, reads=[res("P1"), res("g_rstd16"), res("g_negmr16")],
              writes=[res("norm")], tag=f"c{i}_gnss")

        pl.do("vector", lambda e: e.tensor_tensor(
            out=S["nsc"][:], in0=S["norm"][:], in1=S["gnsc"][:], op=OP.mult),
            reads=[res("norm"), res("gnsc")], writes=[res("nsc")], tag=f"c{i}_nsc")
        def mmot(e, lo, pi):
            last = None
            for b in range(lo, lo + 4):
                reg = P[pi][:, (b % 4) * 128:(b % 4) * 128 + 128]
                e.matmul(reg, S["silu"][:, b * 128:(b + 1) * 128], S["idn"][:],
                         start=True, stop=False)
                last = e.matmul(reg, S["nsc"][:, b * 128:(b + 1) * 128],
                                S["idn"][:], start=False, stop=True)
            return last
        pl.do("tensor", lambda e: mmot(e, 0, 2),
              reads=[res("silu"), res("nsc"), res("idn")], writes=[res("P2")],
              tag=f"c{i}_oT_t0")
        pl.do("tensor", lambda e: mmot(e, 4, 3),
              reads=[res("silu"), res("nsc"), res("idn")], writes=[res("P3")],
              tag=f"c{i}_oT_t1")
        transpose_evac("outT", f"c{i}_oT", (2, 3))

        def mmw2(e, pb, half):
            last = e.matmul(P[pb][:, :], S["ones1"][:],
                            S["gnbw2"][0:1, half * 512:(half + 1) * 512],
                            start=True, stop=False)
            for b in range(NB):
                last = e.matmul(
                    P[pb][:, :], S["outT"][:, b * 128:(b + 1) * 128],
                    S["w2"][:, b * HID + half * 512: b * HID + (half + 1) * 512],
                    start=False, stop=(b == NB - 1))
            return last
        pl.do("tensor", lambda e: mmw2(e, 6, 0),
              reads=[res("outT"), res("w2"), res("ones1"), res("gnbw2")],
              writes=[res("P6")], tag=f"c{i}_w2a")
        pl.do("tensor", lambda e: mmw2(e, 7, 1),
              reads=[res("outT"), res("w2"), res("ones1"), res("gnbw2")],
              writes=[res("P7")], tag=f"c{i}_w2b")

        ln_stats("", None, from_psum2=(6, 7), tag=f"c{i}_ln2")
        xrs = f"xres{par}"
        for half, pb in ((0, 6), (1, 7)):
            pl.do("scalar", lambda e, half=half, pb=pb: e.activation(
                S["at2"][:, half * 512:(half + 1) * 512], P[pb][:, :], AF.Identity,
                bias=S["negmr"][:], scale=S["rstd"][:]),
                reads=[res(f"P{pb}"), res("rstd"), res("negmr")], writes=[res("at2")],
                tag=f"c{i}_at2{half}")
        pl.do("vector", lambda e, xb=xb, xrs=xrs: e.tensor_tensor(
            out=S[xrs][:], in0=S[xb][:], in1=S["at2"][:], op=OP.add),
            reads=[res(xb), res("at2")], writes=[res(xrs)], tag=f"c{i}_xres")

        ln_stats(xrs, lambda e, xrs=xrs: S[xrs][:], tag=f"c{i}_ln3",
                 sq_dump="xf")
        pl.do("scalar", lambda e, xrs=xrs: e.activation(
            S["xf"][:], S[xrs][:], AF.Identity, bias=S["negmr"][:], scale=S["rstd"][:]),
            reads=[res(xrs), res("rstd"), res("negmr")], writes=[res("xf")],
            tag=f"c{i}_ln3n")
        xfT = f"xfT{par}"
        transpose8("xf", xfT, f"c{i}_xfT", tp=(6, 7))

        if kind == "fullpre":
            return

        xfTp = f"xfT{(i + 1) % 2}"
        first = (i == n_state + 1)

        pl.do("vector", lambda e, xfT=xfT: e.tensor_tensor(
            out=S["dmix"][:].rearrange("p (b t) -> p b t", t=128)[:, :, 1:128],
            in0=S[xfT][:].rearrange("p (b t) -> p b t", t=128)[:, :, 1:128],
            in1=S[xfT][:].rearrange("p (b t) -> p b t", t=128)[:, :, 0:127],
            op=OP.subtract),
            reads=[res(xfT)], writes=[res("dmix")], tag=f"c{i}_dmx")

        def dcol(e, xfT=xfT, xfTp=xfTp, first=first):
            sc1 = S["nmaskp"][:] if first else -1.0
            last = None
            for b in range(NB):
                last = e.scalar_tensor_tensor(
                    out=S["dmix"][:, b * 128:b * 128 + 1],
                    in0=S[xfTp][:, b * 128 + 127:b * 128 + 128],
                    scalar=sc1, in1=S[xfT][:, b * 128:b * 128 + 1],
                    op0=OP.mult, op1=OP.add)
            return last
        pl.do("vector", dcol,
              reads=[res(xfT), res(xfTp)] + ([res("nmaskp")] if first else []),
              writes=[res("dmix")], tag=f"c{i}_dcl", small=True)

        def mix(e, dst, mun, xfT=xfT):
            last = None
            for b in range(NB):
                last = e.scalar_tensor_tensor(
                    out=S[dst][:, b * 128:(b + 1) * 128],
                    in0=S["dmix"][:, b * 128:(b + 1) * 128],
                    scalar=S[mun][:, b:b + 1],
                    in1=S[xfT][:, b * 128:(b + 1) * 128],
                    op0=OP.mult, op1=OP.add)
            return last
        xr, xk = "xrT", "xkT"
        pl.do("vector", lambda e, xr=xr: mix(e, xr, "murn"),
              reads=[res("dmix"), res(xfT), res("murn")], writes=[res(xr)],
              tag=f"c{i}_mr", small=True)
        pl.do("vector", lambda e, xk=xk: mix(e, xk, "mukn"),
              reads=[res("dmix"), res(xfT), res("mukn")], writes=[res(xk)],
              tag=f"c{i}_mk", small=True)

        def mmffn(e, lhsT, w, pb, half):
            last = None
            for b in range(NB):
                last = e.matmul(
                    P[pb][:, :], S[lhsT][:, b * 128:(b + 1) * 128],
                    S[w][:, b * HID + half * 512: b * HID + (half + 1) * 512],
                    start=(b == 0), stop=(b == NB - 1))
            return last
        pl.do("tensor", lambda e, xk=xk: mmffn(e, xk, "wkT", 6, 0),
              reads=[res(xk), res("wkT")], writes=[res("P6")], tag=f"c{i}_kp0")
        pl.do("tensor", lambda e, xk=xk: mmffn(e, xk, "wkT", 7, 1),
              reads=[res(xk), res("wkT")], writes=[res("P7")], tag=f"c{i}_kp1")
        kr = f"krelu{par}"
        for half, pb in ((0, 6), (1, 7)):
            pl.do("scalar", lambda e, half=half, pb=pb, kr=kr: e.activation(
                S[kr][:, half * 512:(half + 1) * 512], P[pb][:, :], AF.Relu),
                reads=[res(f"P{pb}")], writes=[res(kr)], tag=f"c{i}_kr{half}")
        pl.do("tensor", lambda e, xr=xr: mmffn(e, xr, "wrT", 6, 0),
              reads=[res(xr), res("wrT")], writes=[res("P6")], tag=f"c{i}_r0")
        pl.do("tensor", lambda e, xr=xr: mmffn(e, xr, "wrT", 7, 1),
              reads=[res(xr), res("wrT")], writes=[res("P7")], tag=f"c{i}_r1")

        emit_rtanh(i)

    def emit_ffn(i):
        par = i % 2
        th, kr = f"tanh{par}", f"krelu{par}"
        xrs = f"xres{par}"
        transpose_mm(kr, f"c{i}_kT", (4, 5))
        transpose_evac("kactT", f"c{i}_kT", (4, 5), func=AF.Square)
        def mmffn(e, lhsT, w, pb, half):
            last = None
            for b in range(NB):
                last = e.matmul(
                    P[pb][:, :], S[lhsT][:, b * 128:(b + 1) * 128],
                    S[w][:, b * HID + half * 512: b * HID + (half + 1) * 512],
                    start=(b == 0), stop=(b == NB - 1))
            return last
        pl.do("tensor", lambda e: mmffn(e, "kactT", "wvT", 2, 0),
              reads=[res("kactT"), res("wvT")], writes=[res("P2")], tag=f"c{i}_kv0")
        pl.do("tensor", lambda e: mmffn(e, "kactT", "wvT", 3, 1),
              reads=[res("kactT"), res("wvT")], writes=[res("P3")], tag=f"c{i}_kv1")

        for half, pb in ((0, 2), (1, 3)):
            pl.do("vector", lambda e, half=half, pb=pb, th=th: e.scalar_tensor_tensor(
                out=S["fbuf"][:, half * 512:(half + 1) * 512],
                in0=S[th][:, half * 512:(half + 1) * 512], scalar=1.0,
                in1=P[pb][:, :], op0=OP.add, op1=OP.mult),
                reads=[res(th), res(f"P{pb}")], writes=[res("fbuf")],
                tag=f"c{i}_f{half}")

        ln_stats("fbuf", lambda e: S["fbuf"][:], tag=f"c{i}_ln4", ts="2",
                 sq_dump="at2", eps=4.0 * EPS)
        pl.do("scalar", lambda e: e.activation(
            S["at4"][:], S["fbuf"][:], AF.Identity, bias=S["negmr2"][:],
            scale=S["rstd2"][:]),
            reads=[res("fbuf"), res("rstd2"), res("negmr2")], writes=[res("at4")],
            tag=f"c{i}_ln4n")
        ob = f"outb{par}"
        pl.do("vector", lambda e, xrs=xrs, ob=ob: e.tensor_tensor(
            out=S[ob][:], in0=S[xrs][:], in1=S["at4"][:], op=OP.add),
            reads=[res(xrs), res("at4")], writes=[res(ob)], tag=f"c{i}_out")
        j = i - (n_state + 1)
        pl.do("scalar", (lambda e, j=j, ob=ob: e.dma_start(
            d['out'][j * CH:(j + 1) * CH, :], S[ob][:])),
            reads=[res(ob)], dma=True, tag=f"c{i}_odma")

    for i in range(min(2, NCH)):
        pl.do("sync", (lambda e, i=i: e.dma_start(
            S[f"x{i % 3}"][0:64, :], x_src(i)[0:64, :])),
              writes=[res(f"x{i % 3}")], dma=True, tag=f"c{i}_xpf0")
        pl.do("gpsimd", (lambda e, i=i: e.dma_start(
            S[f"x{i % 3}"][64:128, :], x_src(i)[64:128, :])),
              writes=[res(f"x{i % 3}")], dma=True, tag=f"c{i}_xpg0")

    prev_main = None
    emit_ln1(0)
    for i in range(NCH):
        kind = ("state" if i < n_state else
                "fullpre" if i == n_state else "main")
        if i + 1 < NCH:
            emit_ln1(i + 1)
        emit_attn(i, kind)
        if i + 2 < NCH:
            pl.do("sync", (lambda e, i=i: e.dma_start(
                S[f"x{(i + 2) % 3}"][0:64, :], x_src(i + 2)[0:64, :])),
                  writes=[res(f"x{(i + 2) % 3}")], dma=True, tag=f"c{i}_xpf")
            pl.do("gpsimd", (lambda e, i=i: e.dma_start(
                S[f"x{(i + 2) % 3}"][64:128, :], x_src(i + 2)[64:128, :])),
                writes=[res(f"x{(i + 2) % 3}")], dma=True, tag=f"c{i}_xpg")
        if prev_main is not None:
            emit_ffn(prev_main)
        if kind == "main":
            prev_main = i
        else:
            prev_main = None
    if prev_main is not None:
        emit_ffn(prev_main)

    for nm in debug_dump:
        dd = nc.dram_tensor("dbg_" + nm, list(S[nm].shape), S[nm].dtype,
                            kind="ExternalOutput")
        pl.do("sync", (lambda e, nm=nm, dd=dd: e.dma_start(dd[:], S[nm][:])),
              reads=[res(nm)], dma=True, tag=f"dbg_{nm}")

    pl.schedule()
    counts = pl.resolve()

    sems = {}
    sem_guards = []
    for nm in ("sT", "sV", "sS", "sG", "dD", "dG", "dA"):
        g = nc.semaphore(nm)
        sems[nm] = g.__enter__()
        sem_guards.append(g)

    blk_guard = nc.Block()
    blk = blk_guard.__enter__()

    @blk.tensor
    def _(e):
        pl.run("tensor", e, sems, nc, annotate)

    @blk.vector
    def _(e):
        pl.run("vector", e, sems, nc, annotate)

    @blk.scalar
    def _(e):
        pl.run("scalar", e, sems, nc, annotate)
        e.wait_ge(sems["dA"], counts.get("dA", 0))

    @blk.sync
    def _(e):
        pl.run("sync", e, sems, nc, annotate)

    @blk.gpsimd
    def _(e):
        pl.run("gpsimd", e, sems, nc, annotate)

    blk_guard.__exit__(None, None, None)
    for g in reversed(sem_guards):
        g.__exit__(None, None, None)
    return nc



def make_inputs_for_core(core, x_full, shared, n_state=N_STATE, n_main=N_MAIN):
    b, q = core // 4, core % 4
    pre_len = (n_state + 1) * CH
    span = n_main * CH
    start = q * span
    x_pre = np.zeros((pre_len, HID), dtype=np.float32)
    if start > 0:
        n = min(start, pre_len)
        x_pre[pre_len - n:] = x_full[b, start - n:start]
    x_main = np.ascontiguousarray(x_full[b, start:start + span])
    maskp = np.full((128, 1), 0.0 if q == 0 else 1.0, dtype=np.float32)
    m = {"x_pre": x_pre, "x_main": x_main, "maskp": maskp}
    m.update(shared)
    return m


def make_shared_inputs(inp):
    dt, gqE, gqO, lam, g128 = make_host_constants()
    shared = {
        "wq_bd": blkdiag_layout(np.asarray(inp["L0_attn_wq"], np.float32)),
        "wk_bd": blkdiag_layout(np.asarray(inp["L0_attn_wk"], np.float32)),
        "wv_bd": blkdiag_layout(np.asarray(inp["L0_attn_wv"], np.float32)),
        "w1": band_layout(np.asarray(inp["L0_attn_w1"], np.float32)),
        "w2": band_layout(np.asarray(inp["L0_attn_w2"], np.float32)),
        "wrT": band_layout(np.ascontiguousarray(
            np.asarray(inp["L0_ffn_W_r"], np.float32).T)),
        "wkT": band_layout(np.ascontiguousarray(
            np.asarray(inp["L0_ffn_W_k"], np.float32).T)),
        "wvT": band_layout(np.ascontiguousarray(
            np.asarray(inp["L0_ffn_W_v"], np.float32).T)),
        "dt": dt, "gqE": gqE, "gqO": gqO, "lam": lam, "g128": g128,
        "gnsc": np.ascontiguousarray(np.broadcast_to(
            np.asarray(inp["L0_attn_gn_scale"], np.float32)[None, :], (128, HID))),
        "gnb": colwise(np.asarray(inp["L0_attn_gn_bias"], np.float32)),
        "mur": colwise(np.asarray(inp["L0_ffn_mu_r"], np.float32)),
        "muk": colwise(np.asarray(inp["L0_ffn_mu_k"], np.float32)),
        "idn": np.eye(128, dtype=np.float32).astype(BF),
        "ones1": np.ones((1, 128), dtype=np.float32).astype(BF),
    }
    return shared


def kernel(**inputs):
    x_full = np.asarray(inputs["L0_attn_x"], np.float32)
    shared = make_shared_inputs(inputs)
    nc = build_program()
    in_maps = [make_inputs_for_core(c, x_full, shared) for c in range(N_CORES)]
    res = run_bass_kernel_spmd(nc, in_maps, core_ids=list(range(N_CORES)))
    out = np.empty((2, 8192, HID), dtype=np.float32)
    for c in range(N_CORES):
        b, q = c // 4, c % 4
        out[b, q * SPAN:(q + 1) * SPAN] = res.results[c]["out"]
    return out


# version 28
# speedup vs baseline: 1.0265x; 1.0265x over previous
import math
import numpy as np
import ml_dtypes
import concourse.bass as bass
import concourse.mybir as mybir
from concourse.bass_utils import run_bass_kernel_spmd
import re as _re

MEASURED_COST = {
    "gpsimd|ld_dt": 617, "gpsimd|ld_gnsc": 581, "gpsimd|ld_gqE": 644, "gpsimd|ld_gqO": 635,
    "gpsimd|ld_lam": 636, "gpsimd|ld_w1": 637, "gpsimd|ld_w2": 638, "gpsimd|ld_wkT": 607,
    "gpsimd|ld_wk_bd": 649, "gpsimd|ld_wq_bd": 641, "gpsimd|ld_wrT": 644, "gpsimd|ld_wvT": 618,
    "gpsimd|ld_wv_bd": 2288, "gpsimd|sbf": 1870, "gpsimd|sdec": 1270, "scalar|at20": 777,
    "scalar|at21": 669, "scalar|gnss": 2481, "scalar|grs": 1590, "scalar|gsq0": 687,
    "scalar|gsq1": 687, "scalar|kT_es": 579, "scalar|kT_ev": 687, "scalar|kr0": 579,
    "scalar|kr1": 579, "scalar|ln1_rstd": 295, "scalar|ln1n": 1240, "scalar|ln2_rstd": 294,
    "scalar|ln2_sqA": 436, "scalar|ln2_sqB": 380, "scalar|ln3_rstd": 295, "scalar|ln3n": 1239,
    "scalar|ln4_rstd": 1578, "scalar|ln4n": 1239, "scalar|oT_es": 687, "scalar|odma": 1875,
    "scalar|qkv_Vev0": 687, "scalar|qkv_Vev1": 687, "scalar|qkv_kev0": 687, "scalar|qkv_kev1": 687,
    "scalar|silu0": 687, "scalar|silu1": 1970, "scalar|th0": 687, "scalar|th1": 577,
    "scalar|warm_tables": 5490, "scalar|xfT_es": 687, "scalar|xnT_es": 687, "sync|ld_g128": 608,
    "sync|ld_gnb": 650, "sync|ld_idn": 606, "sync|ld_maskp": 1202, "sync|ld_muk": 1045,
    "sync|ld_mur": 611, "sync|ld_ones1": 642, "sync|xpf": 591, "sync|xpf0": 1256,
    "tensor|g0": 1887, "tensor|g1": 1674, "tensor|gnbw2": 6099, "tensor|kT_t0": 307,
    "tensor|kT_t1": 223, "tensor|kp0": 3570, "tensor|kp1": 1725, "tensor|kv0": 1886,
    "tensor|kv1": 1727, "tensor|oT_t0": 1012, "tensor|oT_t1": 853, "tensor|qkv_K0": 426,
    "tensor|qkv_K1": 426, "tensor|qkv_V0": 584, "tensor|qkv_V1": 427, "tensor|qkv_k0": 308,
    "tensor|qkv_k1": 308, "tensor|qkv_q0": 223, "tensor|qkv_q1": 224, "tensor|r0": 1674,
    "tensor|r1": 1727, "tensor|ret0": 626, "tensor|ret1": 1010, "tensor|sc0": 223,
    "tensor|sc1": 308, "tensor|sc2": 223, "tensor|sc3": 307, "tensor|stmm": 639,
    "tensor|w2a": 3618, "tensor|w2b": 2150, "tensor|xfT_t0": 586, "tensor|xfT_t1": 427,
    "tensor|xnT_t0": 585, "tensor|xnT_t1": 427, "vector|dm0": 603, "vector|dm1": 691,
    "vector|dm2": 681, "vector|dm3": 671, "vector|dmix": 693, "vector|f0": 686,
    "vector|f1": 687, "vector|ge": 174, "vector|gm": 81, "vector|gm2": 143,
    "vector|gnbw2_ev0": 678, "vector|gnbw2_ev1": 593, "vector|gnm": 174, "vector|gnv": 2106,
    "vector|grv": 253, "vector|gs0": 601, "vector|gs1": 692, "vector|gs2": 1220,
    "vector|gv": 176, "vector|gve": 175, "vector|ln1_ex2": 61, "vector|ln1_m2": 219,
    "vector|ln1_mean": 153, "vector|ln1_negmr": 215, "vector|ln1_rvar": 70, "vector|ln1_sq": 570,
    "vector|ln1_sum": 568, "vector|ln1_veps": 154, "vector|ln2_cq": 159, "vector|ln2_cs": 159,
    "vector|ln2_ex2": 154, "vector|ln2_m2": 219, "vector|ln2_mean": 154, "vector|ln2_negmr": 214,
    "vector|ln2_rvar": 164, "vector|ln2_sumA": 306, "vector|ln2_sumB": 350, "vector|ln2_veps": 154,
    "vector|ln3_ex2": 154, "vector|ln3_m2": 219, "vector|ln3_mean": 153, "vector|ln3_negmr": 214,
    "vector|ln3_rvar": 164, "vector|ln3_sq": 617, "vector|ln3_sum": 568, "vector|ln3_veps": 154,
    "vector|ln4_ex2": 153, "vector|ln4_m2": 126, "vector|ln4_mean": 60, "vector|ln4_negmr": 214,
    "vector|ln4_rvar": 164, "vector|ln4_sq": 570, "vector|ln4_sum": 568, "vector|ln4_veps": 154,
    "vector|mk": 2100, "vector|mr": 2097, "vector|nsc": 691, "vector|oT_ev": 691,
    "vector|out": 1224, "vector|qkv_Kev0": 691, "vector|qkv_Kev1": 680, "vector|qkv_qtE0": 601,
    "vector|qkv_qtE1": 692, "vector|qkv_qtO0": 598, "vector|qkv_qtO1": 596, "vector|sacc": 687,
    "vector|warm_dve": 1014, "vector|xfT_ev": 691, "vector|xnT_ev": 692, "vector|xp": 915,
    "vector|xres": 1225,
}


F32 = mybir.dt.float32
BF16 = mybir.dt.bfloat16
AF = mybir.ActivationFunctionType
OP = mybir.AluOpType
AX = mybir.AxisListType
BF = ml_dtypes.bfloat16

HID, NH, HD = 1024, 16, 64
NB = HID // 128
NP = NH // 2
CH = 128
EPS = 1e-5

N_CORES = 8
SPAN = 2048
N_MAIN = SPAN // CH
N_STATE = 47
PRE = (N_STATE + 1) * CH


def gammas_f32():
    lin = np.linspace(np.log(1.0 / 512.0), np.log(1.0 / 32.0), NH)
    return (1.0 - np.exp(lin)).astype(np.float32)



ENGINES = ("tensor", "vector", "scalar", "sync", "gpsimd")
ESEM = {"tensor": "sT", "vector": "sV", "scalar": "sS", "gpsimd": "sG"}
QSEM = {"sync": "dD", "gpsimd": "dG", "scalar": "dA"}


class Step:
    __slots__ = ("eng", "fn", "reads", "writes", "dma", "idx",
                 "deps", "inc_val", "needs_inc", "tag", "small")

    def __init__(self, eng, fn, reads, writes, dma, idx, tag, small):
        self.eng, self.fn = eng, fn
        self.reads, self.writes = list(reads), list(writes)
        self.dma = dma
        self.idx = idx
        self.deps = []
        self.inc_val = None
        self.needs_inc = False
        self.tag = tag
        self.small = small


class Res:
    __slots__ = ("name", "last_write", "readers", "exclusive")

    def __init__(self, name, exclusive=False):
        self.name = name
        self.last_write = None
        self.readers = []
        self.exclusive = exclusive


class Plan:
    def __init__(self, safe=False):
        self.steps = []
        self.safe = safe

    def do(self, eng, fn, reads=(), writes=(), dma=False, tag="", small=None):
        if small is None:
            small = False
        st = Step(eng, fn, reads, writes, dma, len(self.steps), tag, small)
        dep = set()
        for r in st.reads:
            if r.last_write is not None:
                dep.add(r.last_write)
            if r.exclusive:
                for rd in r.readers:
                    if rd.eng != eng:
                        dep.add(rd)
        for w in st.writes:
            if w.last_write is not None:
                dep.add(w.last_write)
            for rd in w.readers:
                dep.add(rd)
        st.deps = [d for d in dep if d is not st]
        for r in st.reads:
            r.readers.append(st)
        for w in st.writes:
            w.last_write = st
            w.readers = []
        self.steps.append(st)
        return st

    def schedule(self):
        import heapq

        def cost(st):
            t, e = st.tag, st.eng
            if st.dma:
                return 2000
            mv = MEASURED_COST.get(e + "|" + _re.sub(r"^c\d+_", "", t))
            if mv is not None:
                return mv
            if e == "tensor":
                if "_t0" in t or "_t1" in t:
                    return 600
                if "_q0" in t or "_q1" in t or "_k0" in t or "_k1" in t:
                    return 500
                if "_K0" in t or "_K1" in t or "_V0" in t or "_V1" in t:
                    return 500
                if "_sc" in t:
                    return 350
                if "_ret" in t:
                    return 900
                if "_stmm" in t:
                    return 900
                if ("_g0" in t or "_g1" in t or "_w2" in t or "_r0" in t
                        or "_r1" in t or "_kp" in t or "_kv" in t):
                    return 1900
                return 400
            if e == "scalar":
                if "_rstd" in t:
                    return 500
                if "_ln" in t or "_silu" in t or "_th" in t or "_kr" in t                         or "_at2" in t or "_gsq" in t or "_Vev" in t                         or "_kev" in t or "_es" in t or "gns" in t:
                    return 900
                return 400
            if e == "gpsimd":
                return 2600
            if "_sum" in t or "_sq" in t:
                return 1200
            if ("_qtE" in t or "_qtO" in t or "_Kev" in t or "_dm" in t
                    or "_ev" in t or "_silu" in t or "_f0" in t or "_f1" in t
                    or "_nsc" in t or "_outO" in t or "_xres" in t
                    or "_out" in t or "_ksq" in t):
                return 800
            if "_sdec" in t or "_sacc" in t or "_sbf" in t:
                return 650
            if "_mr" in t or "_mk" in t or "_xp" in t or "_gnv" in t:
                return 1500
            if "_gs" in t:
                return 700
            return 220

        succ = {st: [] for st in self.steps}
        indeg = {st: 0 for st in self.steps}
        for st in self.steps:
            for d in st.deps:
                succ[d].append(st)
                indeg[st] += 1
        prio = {}
        for st in reversed(self.steps):
            prio[st] = cost(st) + max((prio[s] for s in succ[st]), default=0)
        eng_avail = {e: 0.0 for e in ENGINES}
        finish = {}
        ready = []
        for st in self.steps:
            if indeg[st] == 0:
                heapq.heappush(ready, (-prio[st], st.idx, st))
        order = []
        while ready:
            _, _, st = heapq.heappop(ready)
            start = eng_avail[st.eng]
            for d in st.deps:
                hop = 300 if (d.eng != st.eng or d.dma) else 0
                start = max(start, finish[d] + hop)
            f = start + cost(st)
            finish[st] = f
            eng_avail[st.eng] = f
            order.append(st)
            for s in succ[st]:
                indeg[s] -= 1
                if indeg[s] == 0:
                    heapq.heappush(ready, (-prio[s], s.idx, s))
        assert len(order) == len(self.steps)
        self.steps = order
        for k, st in enumerate(self.steps):
            st.idx = k

    def resolve(self):
        def needs_edge(d, st):
            return (d.dma or d.eng != st.eng or self.safe
                    or (d.eng == "vector" and d.small))

        for st in self.steps:
            for d in st.deps:
                if needs_edge(d, st):
                    d.needs_inc = True
        cnt = {}
        for st in self.steps:
            if st.dma:
                sem = QSEM[st.eng]
                cnt[sem] = cnt.get(sem, 0) + 16
                st.inc_val = cnt[sem]
                st.needs_inc = True
            elif st.needs_inc:
                sem = ESEM[st.eng]
                cnt[sem] = cnt.get(sem, 0) + 1
                st.inc_val = cnt[sem]
        marks = {e: {} for e in ENGINES}
        self.waits = {}
        for st in self.steps:
            w = {}
            for d in st.deps:
                if not needs_edge(d, st):
                    continue
                sem = QSEM[d.eng] if d.dma else ESEM[d.eng]
                assert d.inc_val is not None
                if marks[st.eng].get(sem, 0) < d.inc_val:
                    w[sem] = max(w.get(sem, 0), d.inc_val)
            for sem, v in w.items():
                marks[st.eng][sem] = v
            self.waits[st.idx] = w
        return cnt

    def run(self, eng_name, eng, sems, nc=None, annotate=False):
        for st in self.steps:
            if st.eng != eng_name:
                continue
            for sem, v in self.waits[st.idx].items():
                eng.wait_ge(sems[sem], v)
            last = st.fn(eng)
            if st.needs_inc:
                sem = QSEM[st.eng] if st.dma else ESEM[st.eng]
                assert last is not None, f"step {st.tag} returned no instruction"
                last.then_inc(sems[sem], 16 if st.dma else 1)
            if last is not None and annotate:
                try:
                    last.annotate(st.tag)
                except Exception:
                    pass



def make_host_constants():
    g = gammas_f32().astype(np.float64)
    t = np.arange(CH, dtype=np.float64)
    dt = np.zeros((CH, NH * CH))
    gq = np.zeros((CH, NP * CH))
    lam = np.zeros((CH, NP * CH))
    g128 = np.zeros((CH, NP * HD))
    causal = (t[:, None] <= t[None, :]).astype(np.float64)
    for h in range(NH):
        gh = g[h]
        dt[:, h * CH:(h + 1) * CH] = (gh ** (-(t + 1.0)))[:, None] * causal
        p, j = h // 2, h % 2
        gq[j * 64:(j + 1) * 64, p * CH:(p + 1) * CH] = (gh ** (t + 1.0))[None, :]
        lam[:, p * CH + j * 64: p * CH + (j + 1) * 64] = (gh ** (127.0 - t))[:, None]
        g128[j * 64:(j + 1) * 64, p * HD:(p + 1) * HD] = gh ** 128.0
    gqE = gq.copy(); gqE[64:128, :] = 0.0
    gqO = gq.copy(); gqO[0:64, :] = 0.0
    return (dt.astype(np.float32), gqE.astype(np.float32),
            gqO.astype(np.float32), lam.astype(np.float32),
            g128.astype(np.float32))


def band_layout(w):
    n = w.shape[1]
    return np.ascontiguousarray(
        w.reshape(NB, 128, n).transpose(1, 0, 2).reshape(128, NB * n))


def blkdiag_layout(wh):
    out = np.zeros((128, NP * 128), dtype=wh.dtype)
    for q in range(NP):
        out[0:64, q * 128:q * 128 + 64] = wh[2 * q]
        out[64:128, q * 128 + 64:q * 128 + 128] = wh[2 * q + 1]
    return out


def colwise(v):
    return np.ascontiguousarray(v.reshape(NB, 128).T)



def build_program(n_state=N_STATE, n_main=N_MAIN, safe=False, debug_dump=(), annotate=False):
    nc = bass.Bass("TRN2", target_bir_lowering=False, debug=False,
                   detect_race_conditions=False)
    pre_len = (n_state + 1) * CH
    span = n_main * CH
    NCH = n_state + 1 + n_main

    d = {}
    d['x_pre'] = nc.dram_tensor("x_pre", [pre_len, HID], F32, kind="ExternalInput")
    d['x_main'] = nc.dram_tensor("x_main", [span, HID], F32, kind="ExternalInput")
    d['maskp'] = nc.dram_tensor("maskp", [128, 1], F32, kind="ExternalInput")
    for nm in ("wq_bd", "wk_bd", "wv_bd"):
        d[nm] = nc.dram_tensor(nm, [128, NP * 128], F32, kind="ExternalInput")
    for nm in ("w1", "w2", "wrT", "wkT", "wvT"):
        d[nm] = nc.dram_tensor(nm, [128, NB * HID], F32, kind="ExternalInput")
    d['dt'] = nc.dram_tensor("dt", [128, NH * CH], F32, kind="ExternalInput")
    d['gqE'] = nc.dram_tensor("gqE", [128, NP * CH], F32, kind="ExternalInput")
    d['gqO'] = nc.dram_tensor("gqO", [128, NP * CH], F32, kind="ExternalInput")
    d['lam'] = nc.dram_tensor("lam", [128, NP * CH], F32, kind="ExternalInput")
    d['g128'] = nc.dram_tensor("g128", [128, NP * HD], F32, kind="ExternalInput")
    d['gnsc'] = nc.dram_tensor("gnsc", [128, HID], F32, kind="ExternalInput")
    d['gnb'] = nc.dram_tensor("gnb", [128, NB], F32, kind="ExternalInput")
    d['mur'] = nc.dram_tensor("mur", [128, NB], F32, kind="ExternalInput")
    d['muk'] = nc.dram_tensor("muk", [128, NB], F32, kind="ExternalInput")
    d['idn'] = nc.dram_tensor("idn", [128, 128], BF16, kind="ExternalInput")
    d['ones1'] = nc.dram_tensor("ones1", [1, 128], BF16, kind="ExternalInput")
    d['out'] = nc.dram_tensor("out", [span, HID], F32, kind="ExternalOutput")

    S = {}

    def sb(name, shape, dtype):
        S[name] = nc.alloc_sbuf_tensor("s_" + name, shape, dtype)
        return S[name]

    sb("wq", [128, NP * 128], BF16); sb("wk", [128, NP * 128], BF16)
    sb("wv", [128, NP * 128], BF16)
    sb("w1", [128, NB * HID], BF16); sb("w2", [128, NB * HID], BF16)
    sb("wrT", [128, NB * HID], BF16); sb("wkT", [128, NB * HID], BF16)
    sb("wvT", [128, NB * HID], BF16)
    sb("dt", [128, NH * CH], BF16); sb("gqE", [128, NP * CH], BF16)
    sb("gqO", [128, NP * CH], BF16)
    sb("lam", [128, NP * CH], BF16); sb("g128", [128, NP * HD], F32)
    sb("gnsc", [128, HID], BF16); sb("gnb", [128, NB], F32)
    sb("gnbb", [128, NB], BF16)
    sb("mur", [128, NB], F32); sb("muk", [128, NB], F32)
    sb("murn", [128, NB], F32); sb("mukn", [128, NB], F32)
    sb("nmaskp", [128, 1], F32)
    sb("idn", [128, 128], BF16); sb("ones1", [1, 128], BF16)
    sb("maskp", [128, 1], F32)
    sb("gnbw2", [1, HID], BF16)
    sb("sst", [128, NP * HD], F32); sb("sstd", [128, NP * HD], F32)
    sb("sstb", [128, NP * HD], BF16)
    for k in range(3):
        sb(f"x{k}", [128, HID], F32)
    for k in range(2):
        sb(f"xnT{k}", [128, NB * 128], BF16)
        sb(f"xfT{k}", [128, NB * 128], BF16)
        sb(f"xres{k}", [128, HID], F32)
        sb(f"outb{k}", [128, HID], F32)
    sb("qtE", [128, NP * 128], BF16); sb("qtO", [128, NP * 128], BF16)
    sb("kt", [128, NP * 128], BF16)
    for k in range(2):
        sb(f"kk{k}", [128, NP * 128], BF16); sb(f"vv{k}", [128, NP * 128], BF16)
        sb(f"tanh{k}", [128, HID], BF16); sb(f"krelu{k}", [128, HID], BF16)
    sb("xrT", [128, NB * 128], BF16); sb("xkT", [128, NB * 128], BF16)
    sb("at4", [128, HID], F32)
    sb("ds", [128, NH * CH], BF16)
    sb("norm", [128, HID], BF16)
    sb("nsc", [128, HID], BF16)
    sb("silu", [128, HID], BF16)
    sb("outT", [128, NB * 128], BF16)
    sb("at2", [128, HID], F32)
    sb("xf", [128, HID], BF16)
    sb("xn0", [128, HID], BF16); sb("xn1", [128, HID], BF16)
    sb("dmix", [128, NB * 128], BF16)
    sb("fbuf", [128, HID], F32)
    sb("kactT", [128, NB * 128], BF16)
    sb("bnst", [128, 12], F32)
    for ts in ("1", "2"):
        sb("bnst" + ts, [128, 12], F32)
        sb("mv" + ts, [128, 2], F32); sb("sag" + ts, [128, 4], F32)
        sb("mq" + ts, [128, 1], F32)
        sb("g4a" + ts, [128, 4], F32)
        sb("veps" + ts, [128, 1], F32); sb("rvar" + ts, [128, 1], F32)
        sb("rstd" + ts, [128, 1], F32); sb("negmr" + ts, [128, 1], F32)
    sb("g4a", [128, 4], F32)
    sb("sag", [128, 4], F32)
    sb("mv", [128, 2], F32)
    sb("mq", [128, 1], F32)
    sb("veps", [128, 1], F32); sb("rvar", [128, 1], F32)
    sb("rstd", [128, 1], F32); sb("negmr", [128, 1], F32)
    sb("dump", [128, HID], BF16)
    sb("g_sum16", [128, 16], F32); sb("g_sq16", [128, 16], F32)
    sb("g_mean16", [128, 16], F32); sb("g_ex216", [128, 16], F32)
    sb("g_m2", [128, 16], F32); sb("g_var16", [128, 16], F32)
    sb("g_rv16", [128, 16], F32); sb("g_rstd16", [128, 16], F32)
    sb("g_negmr16", [128, 16], F32)

    P = [nc.alloc_psum_tensor(f"p{i}", [128, 512], F32) for i in range(8)]

    R = {}

    def res(name):
        if name not in R:
            R[name] = Res(name)
        return R[name]

    for nm in S:
        res(nm)
    for i in range(8):
        R[f"P{i}"] = Res(f"P{i}", exclusive=True)

    pl = Plan(safe=safe)

    for nm in ("g128", "gnb", "mur", "muk",
               "maskp", "idn", "ones1"):
        pl.do("sync", (lambda e, nm=nm: e.dma_start(S[nm][:], d[nm][:])),
              writes=[res(nm)], dma=True, tag=f"ld_{nm}")
    for nm, dst in (("lam", "lam"), ("wk_bd", "wk"), ("wv_bd", "wv"),
                    ("dt", "dt"), ("gqE", "gqE"), ("gqO", "gqO"), ("gnsc", "gnsc"), ("wq_bd", "wq"),
                    ("w1", "w1"), ("w2", "w2"), ("wrT", "wrT"),
                    ("wkT", "wkT"), ("wvT", "wvT")):
        pl.do("gpsimd", (lambda e, nm=nm, dst=dst: e.dma_start(S[dst][:], d[nm][:])),
              writes=[res(dst)], dma=True, tag=f"ld_{nm}")
    def warm_tables(e):
        last = None
        for fn in (AF.Silu, AF.Tanh, AF.Sqrt):
            for _ in range(12):
                last = e.activation(S["g4a"][:, 0:1], S["maskp"][:], fn)
        for _ in range(4):
            last = e.activation(S["g4a"][:, 0:1], S["maskp"][:], AF.Square,
                                accum_out=S["g4a"][:, 1:2])
        return last
    pl.do("scalar", warm_tables, reads=[res("maskp")], writes=[res("g4a")],
          tag="warm_tables")

    def warm_dve_accum(e):
        last = None
        for _ in range(4):
            last = e.tensor_scalar(out=S["sag"][:, 0:1], in0=S["maskp"][:],
                                   scalar1=1.0, scalar2=0.0, op0=OP.mult,
                                   op1=OP.add, accum_out=S["sag"][:, 1:2])
            last = e.scalar_tensor_tensor(out=S["sag"][:, 2:3], in0=S["maskp"][:],
                                          scalar=0.0, in1=S["maskp"][:],
                                          op0=OP.add, op1=OP.mult,
                                          accum_out=S["sag"][:, 3:4])
        return last
    for nm_src, nm_dst in (("mur", "murn"), ("muk", "mukn")):
        pl.do("vector", lambda e, a=nm_src, b=nm_dst: e.tensor_scalar(
            out=S[b][:], in0=S[a][:], scalar1=1.0, scalar2=-1.0,
            op0=OP.mult, op1=OP.add),
            reads=[res(nm_src)], writes=[res(nm_dst)], tag=f"mk_{nm_dst}",
            small=True)
    pl.do("vector", lambda e: e.tensor_scalar(
        out=S["nmaskp"][:], in0=S["maskp"][:], scalar1=-1.0, scalar2=None,
        op0=OP.mult),
        reads=[res("maskp")], writes=[res("nmaskp")], tag="mk_nmaskp",
        small=True)
    pl.do("vector", warm_dve_accum, reads=[res("maskp")], writes=[res("sag")],
          tag="warm_dve", small=True)
    pl.do("vector", lambda e: e.memset(S["sst"][:], 0.0), writes=[res("sst")])
    pl.do("vector", lambda e: e.memset(S["sstb"][:], 0.0), writes=[res("sstb")])
    pl.do("vector", lambda e: e.tensor_copy(S["gnbb"][:], S["gnb"][:]),
          reads=[res("gnb")], writes=[res("gnbb")])

    def mm_gnbw2(e):
        last = None
        for half in range(2):
            for b in range(NB):
                last = e.matmul(P[6 + half][0:1, 0:512],
                                S["gnbb"][:, b:b + 1],
                                S["w2"][:, b * HID + half * 512: b * HID + (half + 1) * 512],
                                start=(b == 0), stop=(b == NB - 1))
        return last
    pl.do("tensor", mm_gnbw2, reads=[res("gnbb"), res("w2")],
          writes=[res("P6"), res("P7")], tag="gnbw2")
    pl.do("vector", lambda e: e.tensor_copy(S["gnbw2"][0:1, 0:512], P[6][0:1, 0:512]),
          reads=[res("P6")], writes=[res("gnbw2")], tag="gnbw2_ev0")
    pl.do("vector", lambda e: e.tensor_copy(S["gnbw2"][0:1, 512:1024], P[7][0:1, 0:512]),
          reads=[res("P7")], writes=[res("gnbw2")], tag="gnbw2_ev1")

    def x_src(i):
        if i < n_state + 1:
            return d['x_pre'][i * CH:(i + 1) * CH, :]
        j = i - (n_state + 1)
        return d['x_main'][j * CH:(j + 1) * CH, :]


    def ln_stats(xres_name, src_ap_fn, from_psum2=None, tag="", ts="",
                 sq_dump=None, eps=EPS, sum_eng="vector"):
        MV, VE = "mv" + ts, "veps" + ts
        RV, RS, NM = "rvar" + ts, "rstd" + ts, "negmr" + ts
        dump = S["dump"][:]
        if from_psum2 is None:
            if sum_eng == "scalar":
                pl.do("scalar", lambda e: e.activation(
                    S["dump"][:], src_ap_fn(e), AF.Identity,
                    accum_out=S[MV][:, 0:1]),
                    reads=[res(xres_name)], writes=[res(MV)],
                    tag=f"{tag}_sum")
            else:
                pl.do("vector", lambda e: e.tensor_scalar(
                    out=dump, in0=src_ap_fn(e), scalar1=1.0, scalar2=0.0,
                    op0=OP.mult, op1=OP.add, accum_out=S[MV][:, 0:1]),
                    reads=[res(xres_name)], writes=[res(MV)],
                    tag=f"{tag}_sum", small=True)
            pl.do("scalar", lambda e: e.activation(
                S["dump"][:], src_ap_fn(e), AF.Square,
                accum_out=S["mq" + ts][:, 0:1]),
                reads=[res(xres_name)], writes=[res("mq" + ts)],
                tag=f"{tag}_sq")
            pl.do("vector", lambda e: e.tensor_scalar(
                out=S["g4a" + ts][:, 0:1], in0=S[MV][:, 0:1],
                scalar1=1.0 / 1024, scalar2=None, op0=OP.mult),
                reads=[res(MV)], writes=[res("g4a" + ts)], tag=f"{tag}_mean",
                small=True)
            pl.do("vector", lambda e: e.scalar_tensor_tensor(
                out=S["g4a" + ts][:, 2:3], in0=S[MV][:, 0:1],
                scalar=S["g4a" + ts][:, 0:1], in1=S["mq" + ts][:, 0:1],
                op0=OP.mult, op1=OP.subtract),
                reads=[res(MV), res("g4a" + ts), res("mq" + ts)],
                writes=[res("g4a" + ts)], tag=f"{tag}_m2", small=True)
            pl.do("vector", lambda e, eps=eps: e.tensor_scalar(
                out=S[VE][:], in0=S["g4a" + ts][:, 2:3],
                scalar1=-1.0 / 1024, scalar2=eps, op0=OP.mult, op1=OP.add),
                reads=[res("g4a" + ts)], writes=[res(VE)], tag=f"{tag}_veps",
                small=True)
            pl.do("vector", lambda e: e.reciprocal(S[RV][:], S[VE][:]),
                  reads=[res(VE)], writes=[res(RV)], tag=f"{tag}_rvar",
                  small=True)
            pl.do("scalar", lambda e: e.activation(S[RS][:], S[RV][:], AF.Sqrt),
                  reads=[res(RV)], writes=[res(RS)], tag=f"{tag}_rstd")
            return
        else:
            pa, pb = from_psum2
            pl.do("vector", lambda e: e.tensor_scalar(
                out=dump[:, 0:512], in0=P[pa][:, :], scalar1=1.0, scalar2=0.0,
                op0=OP.mult, op1=OP.add, accum_out=S["sag" + ts][:, 0:1]),
                reads=[res(f"P{pa}")], writes=[res("sag" + ts)],
                tag=f"{tag}_sumA", small=True)
            pl.do("vector", lambda e: e.tensor_scalar(
                out=dump[:, 512:1024], in0=P[pb][:, :], scalar1=1.0, scalar2=0.0,
                op0=OP.mult, op1=OP.add, accum_out=S["sag" + ts][:, 1:2]),
                reads=[res(f"P{pb}")], writes=[res("sag" + ts)],
                tag=f"{tag}_sumB", small=True)
            pl.do("scalar", lambda e: e.activation(
                S["dump"][:, 0:512], P[pa][:, :], AF.Square,
                accum_out=S["sag" + ts][:, 2:3]),
                reads=[res(f"P{pa}")], writes=[res("sag" + ts)],
                tag=f"{tag}_sqA")
            pl.do("scalar", lambda e: e.activation(
                S["dump"][:, 512:1024], P[pb][:, :], AF.Square,
                accum_out=S["sag" + ts][:, 3:4]),
                reads=[res(f"P{pb}")], writes=[res("sag" + ts)],
                tag=f"{tag}_sqB")
            pl.do("vector", lambda e: e.tensor_tensor(
                out=S[MV][:, 0:1], in0=S["sag" + ts][:, 0:1],
                in1=S["sag" + ts][:, 1:2], op=OP.add),
                reads=[res("sag" + ts)], writes=[res(MV)],
                tag=f"{tag}_cs", small=True)
            pl.do("vector", lambda e: e.tensor_tensor(
                out=S[MV][:, 1:2], in0=S["sag" + ts][:, 2:3],
                in1=S["sag" + ts][:, 3:4], op=OP.add),
                reads=[res("sag" + ts)], writes=[res(MV)],
                tag=f"{tag}_cq", small=True)
        pl.do("vector", lambda e: e.tensor_scalar(
            out=S[MV][:, 0:1].tensor[:, 0:1] if False else S["g4a" + ts][:, 0:1],
            in0=S[MV][:, 0:1], scalar1=1.0 / 1024, scalar2=None, op0=OP.mult),
            reads=[res(MV)], writes=[res("g4a" + ts)], tag=f"{tag}_mean",
            small=True)
        pl.do("vector", lambda e: e.tensor_scalar(
            out=S["g4a" + ts][:, 1:2], in0=S[MV][:, 1:2], scalar1=1.0 / 1024,
            scalar2=None, op0=OP.mult),
            reads=[res(MV)], writes=[res("g4a" + ts)], tag=f"{tag}_ex2",
            small=True)
        pl.do("vector", lambda e: e.scalar_tensor_tensor(
            out=S["g4a" + ts][:, 2:3], in0=S["g4a" + ts][:, 0:1],
            scalar=S["g4a" + ts][:, 0:1], in1=S["g4a" + ts][:, 1:2],
            op0=OP.mult, op1=OP.subtract),
            reads=[res("g4a" + ts)], writes=[res("g4a" + ts)], tag=f"{tag}_m2",
            small=True)
        pl.do("vector", lambda e, eps=eps: e.tensor_scalar(
            out=S[VE][:], in0=S["g4a" + ts][:, 2:3], scalar1=-1.0, scalar2=eps,
            op0=OP.mult, op1=OP.add),
            reads=[res("g4a" + ts)], writes=[res(VE)], tag=f"{tag}_veps",
            small=True)
        pl.do("vector", lambda e: e.reciprocal(S[RV][:], S[VE][:]),
              reads=[res(VE)], writes=[res(RV)], tag=f"{tag}_rvar", small=True)
        pl.do("scalar", lambda e: e.activation(S[RS][:], S[RV][:], AF.Sqrt),
              reads=[res(RV)], writes=[res(RS)], tag=f"{tag}_rstd")
        pl.do("vector", lambda e: e.tensor_scalar(
            out=S[NM][:], in0=S["g4a" + ts][:, 0:1], scalar1=S[RS][:],
            scalar2=-1.0, op0=OP.mult, op1=OP.mult),
            reads=[res("g4a" + ts), res(RS)], writes=[res(NM)],
            tag=f"{tag}_negmr", small=True)

    def transpose_mm(src_name, tag, tp=(0, 1), start=True, stop=True):
        pa, pb = tp

        def mmt(e, lo, pi, start, stop):
            last = None
            for b in range(lo, lo + 4):
                last = e.matmul(P[pi][:, (b % 4) * 128:(b % 4) * 128 + 128],
                                S[src_name][:, b * 128:(b + 1) * 128],
                                S["idn"][:], start=start, stop=stop)
            return last
        pl.do("tensor", lambda e: mmt(e, 0, pa, start, stop),
              reads=[res(src_name), res("idn")],
              writes=[res(f"P{pa}")], tag=f"{tag}_t0")
        pl.do("tensor", lambda e: mmt(e, 4, pb, start, stop),
              reads=[res(src_name), res("idn")],
              writes=[res(f"P{pb}")], tag=f"{tag}_t1")

    def transpose_evac(dst_name, tag, tp=(0, 1), func=None):
        pa, pb = tp
        if func is None:
            pl.do("vector", lambda e: e.tensor_copy(S[dst_name][:, 0:512],
                                                    P[pa][:, :]),
                  reads=[res(f"P{pa}")], writes=[res(dst_name)], tag=f"{tag}_ev")
        else:
            pl.do("scalar", lambda e: e.activation(S[dst_name][:, 0:512],
                                                   P[pa][:, :], func),
                  reads=[res(f"P{pa}")], writes=[res(dst_name)], tag=f"{tag}_ev")
        pl.do("scalar", lambda e: e.activation(S[dst_name][:, 512:1024], P[pb][:, :],
                                               func or AF.Identity),
              reads=[res(f"P{pb}")], writes=[res(dst_name)], tag=f"{tag}_es")

    def transpose8(src_name, dst_name, tag, tp=(0, 1)):
        transpose_mm(src_name, tag, tp)
        transpose_evac(dst_name, tag, tp)

    def qkv_proj(xnT_name, want_qt_kt, tag, kkn, vvn, bk=(2, 3), bv=(4, 5)):
        if want_qt_kt:
            def mmw(e, w_name, pb, lo, hi):
                last = None
                for p in range(lo, hi):
                    last = e.matmul(P[pb][:, (p - lo) * 128:(p - lo) * 128 + 128],
                                    S[w_name][:, p * 128:(p + 1) * 128],
                                    S[xnT_name][:, p * 128:(p + 1) * 128],
                                    start=True, stop=True)
                return last
            pl.do("tensor", lambda e: mmw(e, "wq", 2, 0, 4),
                  reads=[res("wq"), res(xnT_name)], writes=[res("P2")], tag=f"{tag}_q0")
            pl.do("tensor", lambda e: mmw(e, "wq", 3, 4, 8),
                  reads=[res("wq"), res(xnT_name)], writes=[res("P3")], tag=f"{tag}_q1")
            pl.do("tensor", lambda e: mmw(e, "wk", 4, 0, 4),
                  reads=[res("wk"), res(xnT_name)], writes=[res("P4")], tag=f"{tag}_k0")
            pl.do("tensor", lambda e: mmw(e, "wk", 5, 4, 8),
                  reads=[res("wk"), res(xnT_name)], writes=[res("P5")], tag=f"{tag}_k1")
            for half, pb in ((0, 2), (1, 3)):
                for which in ("qtE", "qtO"):
                    pl.do("vector", lambda e, half=half, pb=pb, which=which:
                          e.tensor_tensor(
                        out=S[which][:, half * 512:(half + 1) * 512],
                        in0=P[pb][:, :],
                        in1=S["gq" + which[-1]][:, half * 512:(half + 1) * 512],
                        op=OP.mult),
                        reads=[res(f"P{pb}"), res("gq" + which[-1])],
                        writes=[res(which)], tag=f"{tag}_{which}{half}")
            for half, pb in ((0, 4), (1, 5)):
                pl.do("scalar", lambda e, half=half, pb=pb: e.activation(
                    S["kt"][:, half * 512:(half + 1) * 512], P[pb][:, :], AF.Identity),
                    reads=[res(f"P{pb}")], writes=[res("kt")], tag=f"{tag}_kev{half}")

        def mmKV(e, w_name, pb, lo, hi):
            last = None
            for p in range(lo, hi):
                last = e.matmul(P[pb][:, (p - lo) * 128:(p - lo) * 128 + 128],
                                S[xnT_name][:, p * 128:(p + 1) * 128],
                                S[w_name][:, p * 128:(p + 1) * 128],
                                start=True, stop=True)
            return last
        pl.do("tensor", lambda e: mmKV(e, "wk", bk[0], 0, 4),
              reads=[res("wk"), res(xnT_name)], writes=[res(f"P{bk[0]}")], tag=f"{tag}_K0")
        pl.do("tensor", lambda e: mmKV(e, "wk", bk[1], 4, 8),
              reads=[res("wk"), res(xnT_name)], writes=[res(f"P{bk[1]}")], tag=f"{tag}_K1")
        pl.do("tensor", lambda e: mmKV(e, "wv", bv[0], 0, 4),
              reads=[res("wv"), res(xnT_name)], writes=[res(f"P{bv[0]}")], tag=f"{tag}_V0")
        pl.do("tensor", lambda e: mmKV(e, "wv", bv[1], 4, 8),
              reads=[res("wv"), res(xnT_name)], writes=[res(f"P{bv[1]}")], tag=f"{tag}_V1")
        for half, pb in ((0, bk[0]), (1, bk[1])):
            pl.do("vector", lambda e, half=half, pb=pb: e.tensor_tensor(
                out=S[kkn][:, half * 512:(half + 1) * 512], in0=P[pb][:, :],
                in1=S["lam"][:, half * 512:(half + 1) * 512], op=OP.mult),
                reads=[res(f"P{pb}"), res("lam")], writes=[res(kkn)],
                tag=f"{tag}_Kev{half}")
        for half, pb in ((0, bv[0]), (1, bv[1])):
            pl.do("scalar", lambda e, half=half, pb=pb: e.activation(
                S[vvn][:, half * 512:(half + 1) * 512], P[pb][:, :], AF.Identity),
                reads=[res(f"P{pb}")], writes=[res(vvn)], tag=f"{tag}_Vev{half}")

    def state_update(tag, kkn, vvn, st_bank=0, want_sbf=True):
        def mmst(e):
            last = None
            for p in range(NP):
                for j in range(2):
                    last = e.matmul(
                        P[st_bank][j * 64:(j + 1) * 64, p * 64:(p + 1) * 64],
                        S[kkn][:, p * 128 + j * 64: p * 128 + (j + 1) * 64],
                        S[vvn][:, p * 128 + j * 64: p * 128 + (j + 1) * 64],
                        start=True, stop=True)
            return last
        pl.do("tensor", mmst, reads=[res(kkn), res(vvn)], writes=[res(f"P{st_bank}")],
              tag=f"{tag}_stmm")
        pl.do("gpsimd", lambda e: e.tensor_tensor(
            out=S["sstd"][:], in0=S["sst"][:], in1=S["g128"][:], op=OP.mult),
            reads=[res("sst"), res("g128")], writes=[res("sstd")], tag=f"{tag}_sdec")
        pl.do("vector", lambda e: e.tensor_tensor(
            out=S["sst"][:], in0=S["sstd"][:], in1=P[st_bank][:, 0:NP * HD], op=OP.add),
            reads=[res("sstd"), res(f"P{st_bank}")], writes=[res("sst")],
            tag=f"{tag}_sacc", small=True)
        if want_sbf:
            pl.do("gpsimd", lambda e: e.tensor_copy(S["sstb"][:], S["sst"][:]),
                  reads=[res("sst")], writes=[res("sstb")], tag=f"{tag}_sbf")


    def emit_ln1(i):
        xb = f"x{i % 3}"
        xnm = f"xn{i % 2}"
        state = i < n_state
        ln_stats(xb, lambda e, xb=xb: S[xb][:], tag=f"c{i}_ln1", ts="1",
                 sq_dump=xnm, sum_eng="scalar" if state else "vector")
        if state:
            pl.do("vector", lambda e: e.scalar_tensor_tensor(
                out=S["negmr1"][:], in0=S["g4a1"][:, 0:1], scalar=-1.0,
                in1=S["rstd1"][:], op0=OP.mult, op1=OP.mult),
                reads=[res("g4a1"), res("rstd1")], writes=[res("negmr1")],
                tag=f"c{i}_ln1nm", small=True)
            pl.do("scalar", lambda e, xb=xb, xnm=xnm: e.activation(
                S[xnm][:], S[xb][:], AF.Identity, bias=S["negmr1"][:],
                scale=S["rstd1"][:]),
                reads=[res(xb), res("rstd1"), res("negmr1")], writes=[res(xnm)],
                tag=f"c{i}_ln1n")
        else:
            pl.do("vector", lambda e, xb=xb, xnm=xnm: e.tensor_scalar(
                out=S[xnm][:], in0=S[xb][:], scalar1=S["g4a1"][:, 0:1],
                scalar2=S["rstd1"][:], op0=OP.subtract, op1=OP.mult),
                reads=[res(xb), res("rstd1"), res("g4a1")], writes=[res(xnm)],
                tag=f"c{i}_ln1n")

    def emit_rtanh(j):
        th = f"tanh{j % 2}"
        for half, pb in ((0, 6), (1, 7)):
            pl.do("scalar", lambda e, half=half, pb=pb, th=th: e.activation(
                S[th][:, half * 512:(half + 1) * 512], P[pb][:, :], AF.Tanh,
                scale=0.5), reads=[res(f"P{pb}")], writes=[res(th)],
                tag=f"c{j}_th{half}")

    def emit_attn(i, kind):
        par = i % 2
        xb = f"x{i % 3}"
        xnm = f"xn{par}"
        xnT = f"xnT{par}"
        kkn, vvn = f"kk{par}", f"vv{par}"
        full = kind != "state"
        if full:
            tp, bk, bv, stb = (0, 1), (2, 3), (4, 5), 3
        else:
            tp = (0, 1) if par == 0 else (4, 5)
            bk = (2, 3) if par == 0 else (6, 7)
            bv = (4, 5) if par == 0 else (0, 1)
            stb = 6 if par == 0 else 2

        transpose8(xnm, xnT, f"c{i}_xnT", tp)
        qkv_proj(xnT, want_qt_kt=full, tag=f"c{i}_qkv", kkn=kkn, vvn=vvn,
                 bk=bk, bv=bv)

        if full:
            def mmgate(e, pb, half):
                last = None
                for b in range(NB):
                    last = e.matmul(
                        P[pb][:, :],
                        S[xnT][:, b * 128:(b + 1) * 128],
                        S["w1"][:, b * HID + half * 512: b * HID + (half + 1) * 512],
                        start=(b == 0), stop=(b == NB - 1))
                return last
            pl.do("tensor", lambda e: mmgate(e, 4, 0), reads=[res(xnT), res("w1")],
                  writes=[res("P4")], tag=f"c{i}_g0")
            pl.do("tensor", lambda e: mmgate(e, 5, 1), reads=[res(xnT), res("w1")],
                  writes=[res("P5")], tag=f"c{i}_g1")
            for half, pb in ((0, 4), (1, 5)):
                pl.do("scalar", lambda e, half=half, pb=pb: e.activation(
                    S["silu"][:, half * 512:(half + 1) * 512], P[pb][:, :], AF.Silu),
                    reads=[res(f"P{pb}")], writes=[res("silu")], tag=f"c{i}_silu{half}")

            for rnd in range(4):
                pb = (0, 1, 0, 1)[rnd]
                heads = range(rnd * 4, rnd * 4 + 4)

                def mmsc(e, pb=pb, heads=tuple(heads)):
                    last = None
                    for n, h in enumerate(heads):
                        p, j = h // 2, h % 2
                        qn = "qtE" if j == 0 else "qtO"
                        last = e.matmul(
                            P[pb][:, n * 128:(n + 1) * 128],
                            S["kt"][:, p * 128:(p + 1) * 128],
                            S[qn][:, p * 128:(p + 1) * 128],
                            start=True, stop=True)
                    return last
                pl.do("tensor", mmsc, reads=[res("kt"), res("qtE"), res("qtO")],
                      writes=[res(f"P{pb}")], tag=f"c{i}_sc{rnd}")
                pl.do("vector", lambda e, rnd=rnd, pb=pb: e.tensor_tensor(
                    out=S["ds"][:, rnd * 512:(rnd + 1) * 512], in0=P[pb][:, :],
                    in1=S["dt"][:, rnd * 512:(rnd + 1) * 512], op=OP.mult),
                    reads=[res(f"P{pb}"), res("dt")], writes=[res("ds")],
                    tag=f"c{i}_dm{rnd}")

            def mmret(e, pb, lo):
                last = None
                for h in range(lo, lo + 8):
                    p, j = h // 2, h % 2
                    qn = "qtE" if j == 0 else "qtO"
                    c = (h - lo) * 64
                    last = e.matmul(P[pb][:, c:c + 64],
                                    S["ds"][:, h * 128:(h + 1) * 128],
                                    S[vvn][:, p * 128 + j * 64: p * 128 + (j + 1) * 64],
                                    start=True, stop=False)
                    last = e.matmul(P[pb][:, c:c + 64],
                                    S[qn][:, p * 128:(p + 1) * 128],
                                    S["sstb"][:, p * 64:(p + 1) * 64],
                                    start=False, stop=True)
                return last
            pl.do("tensor", lambda e: mmret(e, 0, 0),
                  reads=[res("ds"), res(vvn), res("qtE"), res("qtO"), res("sstb")],
                  writes=[res("P0")], tag=f"c{i}_ret0")
            pl.do("tensor", lambda e: mmret(e, 1, 8),
                  reads=[res("ds"), res(vvn), res("qtE"), res("qtO"), res("sstb")],
                  writes=[res("P1")], tag=f"c{i}_ret1")

        state_update(f"c{i}", kkn, vvn, stb,
                     want_sbf=(full or i + 1 >= n_state))
        if not full:
            return

        pl.do("vector", lambda e: e.reduce_sum(
            S["g_sum16"][:, 0:8], P[0][:, :].rearrange("p (h d) -> p h d", d=64),
            axis=AX.X), reads=[res("P0")], writes=[res("g_sum16")],
            tag=f"c{i}_gs0", small=True)
        pl.do("vector", lambda e: e.reduce_sum(
            S["g_sum16"][:, 8:16], P[1][:, :].rearrange("p (h d) -> p h d", d=64),
            axis=AX.X), reads=[res("P1")], writes=[res("g_sum16")],
            tag=f"c{i}_gs1", small=True)
        pl.do("scalar", lambda e: e.activation(
            S["ds"][:].bitcast(F32)[:, 0:512], P[0][:, :], AF.Square),
              reads=[res("P0")], writes=[res("ds")], tag=f"c{i}_gsq0")
        pl.do("scalar", lambda e: e.activation(
            S["ds"][:].bitcast(F32)[:, 512:1024], P[1][:, :], AF.Square),
              reads=[res("P1")], writes=[res("ds")], tag=f"c{i}_gsq1")
        pl.do("vector", lambda e: e.reduce_sum(
            S["g_sq16"][:], S["ds"][:].bitcast(F32).rearrange(
                "p (h d) -> p h d", d=64),
            axis=AX.X), reads=[res("ds")], writes=[res("g_sq16")],
            tag=f"c{i}_gs2", small=True)
        pl.do("vector", lambda e: e.tensor_scalar(
            out=S["g_mean16"][:], in0=S["g_sum16"][:], scalar1=1.0 / 64, scalar2=None,
            op0=OP.mult), reads=[res("g_sum16")], writes=[res("g_mean16")],
            tag=f"c{i}_gm", small=True)
        pl.do("vector", lambda e: e.tensor_scalar(
            out=S["g_ex216"][:], in0=S["g_sq16"][:], scalar1=1.0 / 64, scalar2=None,
            op0=OP.mult), reads=[res("g_sq16")], writes=[res("g_ex216")],
            tag=f"c{i}_ge", small=True)
        pl.do("vector", lambda e: e.tensor_tensor(
            out=S["g_m2"][:], in0=S["g_mean16"][:], in1=S["g_mean16"][:], op=OP.mult),
            reads=[res("g_mean16")], writes=[res("g_m2")], tag=f"c{i}_gm2", small=True)
        pl.do("vector", lambda e: e.tensor_tensor(
            out=S["g_var16"][:], in0=S["g_ex216"][:], in1=S["g_m2"][:],
            op=OP.subtract),
            reads=[res("g_ex216"), res("g_m2")], writes=[res("g_var16")],
            tag=f"c{i}_gv", small=True)
        pl.do("vector", lambda e: e.tensor_scalar(
            out=S["g_m2"][:], in0=S["g_var16"][:], scalar1=EPS, scalar2=None,
            op0=OP.add), reads=[res("g_var16")], writes=[res("g_m2")],
            tag=f"c{i}_gve", small=True)
        pl.do("vector", lambda e: e.reciprocal(S["g_rv16"][:], S["g_m2"][:]),
              reads=[res("g_m2")], writes=[res("g_rv16")], tag=f"c{i}_grv", small=True)
        pl.do("scalar", lambda e: e.activation(S["g_rstd16"][:], S["g_rv16"][:],
                                               AF.Sqrt),
              reads=[res("g_rv16")], writes=[res("g_rstd16")], tag=f"c{i}_grs")
        pl.do("vector", lambda e: e.scalar_tensor_tensor(
            out=S["g_negmr16"][:], in0=S["g_mean16"][:], scalar=-1.0,
            in1=S["g_rstd16"][:], op0=OP.mult, op1=OP.mult),
            reads=[res("g_mean16"), res("g_rstd16")], writes=[res("g_negmr16")],
            tag=f"c{i}_gnm", small=True)

        def gnv(e):
            last = None
            for h in range(8):
                last = e.tensor_scalar(
                    out=S["norm"][:, h * 64:(h + 1) * 64],
                    in0=P[0][:, h * 64:(h + 1) * 64],
                    scalar1=S["g_rstd16"][:, h:h + 1],
                    scalar2=S["g_negmr16"][:, h:h + 1], op0=OP.mult, op1=OP.add)
            return last
        pl.do("vector", gnv, reads=[res("P0"), res("g_rstd16"), res("g_negmr16")],
              writes=[res("norm")], tag=f"c{i}_gnv", small=True)

        def gns(e):
            last = None
            for h in range(8, 16):
                last = e.activation(
                    S["norm"][:, h * 64:(h + 1) * 64],
                    P[1][:, (h - 8) * 64:(h - 7) * 64], AF.Identity,
                    bias=S["g_negmr16"][:, h:h + 1], scale=S["g_rstd16"][:, h:h + 1])
            return last
        pl.do("scalar", gns, reads=[res("P1"), res("g_rstd16"), res("g_negmr16")],
              writes=[res("norm")], tag=f"c{i}_gnss")

        pl.do("vector", lambda e: e.tensor_tensor(
            out=S["nsc"][:], in0=S["norm"][:], in1=S["gnsc"][:], op=OP.mult),
            reads=[res("norm"), res("gnsc")], writes=[res("nsc")], tag=f"c{i}_nsc")
        def mmot(e, lo, pi):
            last = None
            for b in range(lo, lo + 4):
                reg = P[pi][:, (b % 4) * 128:(b % 4) * 128 + 128]
                e.matmul(reg, S["silu"][:, b * 128:(b + 1) * 128], S["idn"][:],
                         start=True, stop=False)
                last = e.matmul(reg, S["nsc"][:, b * 128:(b + 1) * 128],
                                S["idn"][:], start=False, stop=True)
            return last
        pl.do("tensor", lambda e: mmot(e, 0, 2),
              reads=[res("silu"), res("nsc"), res("idn")], writes=[res("P2")],
              tag=f"c{i}_oT_t0")
        pl.do("tensor", lambda e: mmot(e, 4, 3),
              reads=[res("silu"), res("nsc"), res("idn")], writes=[res("P3")],
              tag=f"c{i}_oT_t1")
        transpose_evac("outT", f"c{i}_oT", (2, 3))

        def mmw2(e, pb, half):
            last = e.matmul(P[pb][:, :], S["ones1"][:],
                            S["gnbw2"][0:1, half * 512:(half + 1) * 512],
                            start=True, stop=False)
            for b in range(NB):
                last = e.matmul(
                    P[pb][:, :], S["outT"][:, b * 128:(b + 1) * 128],
                    S["w2"][:, b * HID + half * 512: b * HID + (half + 1) * 512],
                    start=False, stop=(b == NB - 1))
            return last
        pl.do("tensor", lambda e: mmw2(e, 6, 0),
              reads=[res("outT"), res("w2"), res("ones1"), res("gnbw2")],
              writes=[res("P6")], tag=f"c{i}_w2a")
        pl.do("tensor", lambda e: mmw2(e, 7, 1),
              reads=[res("outT"), res("w2"), res("ones1"), res("gnbw2")],
              writes=[res("P7")], tag=f"c{i}_w2b")

        ln_stats("", None, from_psum2=(6, 7), tag=f"c{i}_ln2")
        xrs = f"xres{par}"
        for half, pb in ((0, 6), (1, 7)):
            pl.do("scalar", lambda e, half=half, pb=pb: e.activation(
                S["at2"][:, half * 512:(half + 1) * 512], P[pb][:, :], AF.Identity,
                bias=S["negmr"][:], scale=S["rstd"][:]),
                reads=[res(f"P{pb}"), res("rstd"), res("negmr")], writes=[res("at2")],
                tag=f"c{i}_at2{half}")
        pl.do("vector", lambda e, xb=xb, xrs=xrs: e.tensor_tensor(
            out=S[xrs][:], in0=S[xb][:], in1=S["at2"][:], op=OP.add),
            reads=[res(xb), res("at2")], writes=[res(xrs)], tag=f"c{i}_xres")

        ln_stats(xrs, lambda e, xrs=xrs: S[xrs][:], tag=f"c{i}_ln3",
                 sq_dump="xf")
        pl.do("vector", lambda e, xrs=xrs: e.tensor_scalar(
            out=S["xf"][:], in0=S[xrs][:], scalar1=S["g4a"][:, 0:1],
            scalar2=S["rstd"][:], op0=OP.subtract, op1=OP.mult),
            reads=[res(xrs), res("rstd"), res("g4a")], writes=[res("xf")],
            tag=f"c{i}_ln3n")
        xfT = f"xfT{par}"
        transpose8("xf", xfT, f"c{i}_xfT", tp=(6, 7))

        if kind == "fullpre":
            return

        xfTp = f"xfT{(i + 1) % 2}"
        first = (i == n_state + 1)

        pl.do("vector", lambda e, xfT=xfT: e.tensor_tensor(
            out=S["dmix"][:].rearrange("p (b t) -> p b t", t=128)[:, :, 1:128],
            in0=S[xfT][:].rearrange("p (b t) -> p b t", t=128)[:, :, 1:128],
            in1=S[xfT][:].rearrange("p (b t) -> p b t", t=128)[:, :, 0:127],
            op=OP.subtract),
            reads=[res(xfT)], writes=[res("dmix")], tag=f"c{i}_dmx")

        def dcol(e, xfT=xfT, xfTp=xfTp, first=first):
            sc1 = S["nmaskp"][:] if first else -1.0
            last = None
            for b in range(NB):
                last = e.scalar_tensor_tensor(
                    out=S["dmix"][:, b * 128:b * 128 + 1],
                    in0=S[xfTp][:, b * 128 + 127:b * 128 + 128],
                    scalar=sc1, in1=S[xfT][:, b * 128:b * 128 + 1],
                    op0=OP.mult, op1=OP.add)
            return last
        pl.do("vector", dcol,
              reads=[res(xfT), res(xfTp)] + ([res("nmaskp")] if first else []),
              writes=[res("dmix")], tag=f"c{i}_dcl", small=True)

        def mix(e, dst, mun, xfT=xfT):
            last = None
            for b in range(NB):
                last = e.scalar_tensor_tensor(
                    out=S[dst][:, b * 128:(b + 1) * 128],
                    in0=S["dmix"][:, b * 128:(b + 1) * 128],
                    scalar=S[mun][:, b:b + 1],
                    in1=S[xfT][:, b * 128:(b + 1) * 128],
                    op0=OP.mult, op1=OP.add)
            return last
        xr, xk = "xrT", "xkT"
        pl.do("vector", lambda e, xr=xr: mix(e, xr, "murn"),
              reads=[res("dmix"), res(xfT), res("murn")], writes=[res(xr)],
              tag=f"c{i}_mr", small=True)
        pl.do("vector", lambda e, xk=xk: mix(e, xk, "mukn"),
              reads=[res("dmix"), res(xfT), res("mukn")], writes=[res(xk)],
              tag=f"c{i}_mk", small=True)

        def mmffn(e, lhsT, w, pb, half):
            last = None
            for b in range(NB):
                last = e.matmul(
                    P[pb][:, :], S[lhsT][:, b * 128:(b + 1) * 128],
                    S[w][:, b * HID + half * 512: b * HID + (half + 1) * 512],
                    start=(b == 0), stop=(b == NB - 1))
            return last
        pl.do("tensor", lambda e, xk=xk: mmffn(e, xk, "wkT", 6, 0),
              reads=[res(xk), res("wkT")], writes=[res("P6")], tag=f"c{i}_kp0")
        pl.do("tensor", lambda e, xk=xk: mmffn(e, xk, "wkT", 7, 1),
              reads=[res(xk), res("wkT")], writes=[res("P7")], tag=f"c{i}_kp1")
        kr = f"krelu{par}"
        for half, pb in ((0, 6), (1, 7)):
            pl.do("scalar", lambda e, half=half, pb=pb, kr=kr: e.activation(
                S[kr][:, half * 512:(half + 1) * 512], P[pb][:, :], AF.Relu),
                reads=[res(f"P{pb}")], writes=[res(kr)], tag=f"c{i}_kr{half}")
        pl.do("tensor", lambda e, xr=xr: mmffn(e, xr, "wrT", 6, 0),
              reads=[res(xr), res("wrT")], writes=[res("P6")], tag=f"c{i}_r0")
        pl.do("tensor", lambda e, xr=xr: mmffn(e, xr, "wrT", 7, 1),
              reads=[res(xr), res("wrT")], writes=[res("P7")], tag=f"c{i}_r1")

        emit_rtanh(i)

    def emit_ffn(i):
        par = i % 2
        th, kr = f"tanh{par}", f"krelu{par}"
        xrs = f"xres{par}"
        transpose_mm(kr, f"c{i}_kT", (4, 5))
        transpose_evac("kactT", f"c{i}_kT", (4, 5), func=AF.Square)
        def mmffn(e, lhsT, w, pb, half):
            last = None
            for b in range(NB):
                last = e.matmul(
                    P[pb][:, :], S[lhsT][:, b * 128:(b + 1) * 128],
                    S[w][:, b * HID + half * 512: b * HID + (half + 1) * 512],
                    start=(b == 0), stop=(b == NB - 1))
            return last
        pl.do("tensor", lambda e: mmffn(e, "kactT", "wvT", 2, 0),
              reads=[res("kactT"), res("wvT")], writes=[res("P2")], tag=f"c{i}_kv0")
        pl.do("tensor", lambda e: mmffn(e, "kactT", "wvT", 3, 1),
              reads=[res("kactT"), res("wvT")], writes=[res("P3")], tag=f"c{i}_kv1")

        for half, pb in ((0, 2), (1, 3)):
            pl.do("vector", lambda e, half=half, pb=pb, th=th: e.scalar_tensor_tensor(
                out=S["fbuf"][:, half * 512:(half + 1) * 512],
                in0=S[th][:, half * 512:(half + 1) * 512], scalar=1.0,
                in1=P[pb][:, :], op0=OP.add, op1=OP.mult),
                reads=[res(th), res(f"P{pb}")], writes=[res("fbuf")],
                tag=f"c{i}_f{half}")

        ln_stats("fbuf", lambda e: S["fbuf"][:], tag=f"c{i}_ln4", ts="2",
                 sq_dump="at2", eps=4.0 * EPS)
        pl.do("vector", lambda e: e.tensor_scalar(
            out=S["at4"][:], in0=S["fbuf"][:], scalar1=S["g4a2"][:, 0:1],
            scalar2=S["rstd2"][:], op0=OP.subtract, op1=OP.mult),
            reads=[res("fbuf"), res("rstd2"), res("g4a2")], writes=[res("at4")],
            tag=f"c{i}_ln4n")
        ob = f"outb{par}"
        pl.do("gpsimd", lambda e, xrs=xrs, ob=ob: e.tensor_tensor(
            out=S[ob][:], in0=S[xrs][:], in1=S["at4"][:], op=OP.add),
            reads=[res(xrs), res("at4")], writes=[res(ob)], tag=f"c{i}_out")
        j = i - (n_state + 1)
        pl.do("scalar", (lambda e, j=j, ob=ob: e.dma_start(
            d['out'][j * CH:(j + 1) * CH, :], S[ob][:])),
            reads=[res(ob)], dma=True, tag=f"c{i}_odma")

    for i in range(min(2, NCH)):
        pl.do("sync", (lambda e, i=i: e.dma_start(S[f"x{i % 3}"][:], x_src(i))),
              writes=[res(f"x{i % 3}")], dma=True, tag=f"c{i}_xpf0")

    prev_main = None
    emit_ln1(0)
    for i in range(NCH):
        kind = ("state" if i < n_state else
                "fullpre" if i == n_state else "main")
        if i + 1 < NCH:
            emit_ln1(i + 1)
        emit_attn(i, kind)
        if i + 2 < NCH:
            pl.do("sync", (lambda e, i=i: e.dma_start(S[f"x{(i + 2) % 3}"][:],
                                                      x_src(i + 2))),
                  writes=[res(f"x{(i + 2) % 3}")], dma=True, tag=f"c{i}_xpf")
        if prev_main is not None:
            emit_ffn(prev_main)
        if kind == "main":
            prev_main = i
        else:
            prev_main = None
    if prev_main is not None:
        emit_ffn(prev_main)

    for nm in debug_dump:
        dd = nc.dram_tensor("dbg_" + nm, list(S[nm].shape), S[nm].dtype,
                            kind="ExternalOutput")
        pl.do("sync", (lambda e, nm=nm, dd=dd: e.dma_start(dd[:], S[nm][:])),
              reads=[res(nm)], dma=True, tag=f"dbg_{nm}")

    pl.schedule()
    counts = pl.resolve()

    sems = {}
    sem_guards = []
    for nm in ("sT", "sV", "sS", "sG", "dD", "dG", "dA"):
        g = nc.semaphore(nm)
        sems[nm] = g.__enter__()
        sem_guards.append(g)

    blk_guard = nc.Block()
    blk = blk_guard.__enter__()

    @blk.tensor
    def _(e):
        pl.run("tensor", e, sems, nc, annotate)

    @blk.vector
    def _(e):
        pl.run("vector", e, sems, nc, annotate)

    @blk.scalar
    def _(e):
        pl.run("scalar", e, sems, nc, annotate)
        e.wait_ge(sems["dA"], counts.get("dA", 0))

    @blk.sync
    def _(e):
        pl.run("sync", e, sems, nc, annotate)

    @blk.gpsimd
    def _(e):
        pl.run("gpsimd", e, sems, nc, annotate)

    blk_guard.__exit__(None, None, None)
    for g in reversed(sem_guards):
        g.__exit__(None, None, None)
    return nc



def make_inputs_for_core(core, x_full, shared, n_state=N_STATE, n_main=N_MAIN):
    b, q = core // 4, core % 4
    pre_len = (n_state + 1) * CH
    span = n_main * CH
    start = q * span
    x_pre = np.zeros((pre_len, HID), dtype=np.float32)
    if start > 0:
        n = min(start, pre_len)
        x_pre[pre_len - n:] = x_full[b, start - n:start]
    x_main = np.ascontiguousarray(x_full[b, start:start + span])
    maskp = np.full((128, 1), 0.0 if q == 0 else 1.0, dtype=np.float32)
    m = {"x_pre": x_pre, "x_main": x_main, "maskp": maskp}
    m.update(shared)
    return m


def make_shared_inputs(inp):
    dt, gqE, gqO, lam, g128 = make_host_constants()
    shared = {
        "wq_bd": blkdiag_layout(np.asarray(inp["L0_attn_wq"], np.float32)),
        "wk_bd": blkdiag_layout(np.asarray(inp["L0_attn_wk"], np.float32)),
        "wv_bd": blkdiag_layout(np.asarray(inp["L0_attn_wv"], np.float32)),
        "w1": band_layout(np.asarray(inp["L0_attn_w1"], np.float32)),
        "w2": band_layout(np.asarray(inp["L0_attn_w2"], np.float32)),
        "wrT": band_layout(np.ascontiguousarray(
            np.asarray(inp["L0_ffn_W_r"], np.float32).T)),
        "wkT": band_layout(np.ascontiguousarray(
            np.asarray(inp["L0_ffn_W_k"], np.float32).T)),
        "wvT": band_layout(np.ascontiguousarray(
            np.asarray(inp["L0_ffn_W_v"], np.float32).T)),
        "dt": dt, "gqE": gqE, "gqO": gqO, "lam": lam, "g128": g128,
        "gnsc": np.ascontiguousarray(np.broadcast_to(
            np.asarray(inp["L0_attn_gn_scale"], np.float32)[None, :], (128, HID))),
        "gnb": colwise(np.asarray(inp["L0_attn_gn_bias"], np.float32)),
        "mur": colwise(np.asarray(inp["L0_ffn_mu_r"], np.float32)),
        "muk": colwise(np.asarray(inp["L0_ffn_mu_k"], np.float32)),
        "idn": np.eye(128, dtype=np.float32).astype(BF),
        "ones1": np.ones((1, 128), dtype=np.float32).astype(BF),
    }
    return shared


def kernel(**inputs):
    x_full = np.asarray(inputs["L0_attn_x"], np.float32)
    shared = make_shared_inputs(inputs)
    nc = build_program()
    in_maps = [make_inputs_for_core(c, x_full, shared) for c in range(N_CORES)]
    res = run_bass_kernel_spmd(nc, in_maps, core_ids=list(range(N_CORES)))
    out = np.empty((2, 8192, HID), dtype=np.float32)
    for c in range(N_CORES):
        b, q = c // 4, c % 4
        out[b, q * SPAN:(q + 1) * SPAN] = res.results[c]["out"]
    return out
